# version 04
# baseline (speedup 1.0000x reference)
import numpy as np
import ml_dtypes
from contextlib import ExitStack

import concourse.bass as bass
import concourse.mybir as mybir
from concourse.bass_utils import run_bass_kernel_spmd

F32 = mybir.dt.float32
BF16 = mybir.dt.bfloat16
AF = mybir.ActivationFunctionType
ALU = mybir.AluOpType
AXX = mybir.AxisListType.X

HID = 4096
KCONV = 4
INTER = 8192
CONVC = 10240
PROJ = 18560
EPS = 1e-5

TOK = 1024
TOKE = TOK + 3
CS = 128
NCHK = 8
NKT = 32
NCH = 145
NCC = 64
NXBC = 80
TT1 = [343, 343, 341]


class Sem:
    def __init__(self, nc, st, name):
        self.h = st.enter_context(nc.semaphore(name))
        self.n = 0

    def next(self, k=1):
        self.n += k
        return (self.h, self.n)

    def cur(self):
        return (self.h, self.n)


class Eng:

    def __init__(self, name, auto=None):
        self.name = name
        self.ops = []
        self.auto = auto
        self.last = None

    def wait(self, tok):
        if tok is not None:
            self.ops.append(lambda e, t=tok: e.wait_ge(t[0], t[1]))

    def run(self, fn, inc=None, k=1):
        if self.auto is not None:
            if self.last is not None:
                self.wait(self.last)
            if inc is None:
                inc, k = self.auto, 1
        tok = inc.next(k) if inc is not None else None
        if inc is not None:
            self.ops.append(lambda e, f=fn, sh=inc.h, kk=k: f(e).then_inc(sh, kk))
        else:
            self.ops.append(lambda e, f=fn: f(e))
        if self.auto is not None:
            self.last = tok
        return tok


def emit_block(nc, engs):
    with nc.Block() as blk:
        for eng in engs:
            if not eng.ops:
                continue
            dec = getattr(blk, eng.name)
            ops = list(eng.ops)
            eng.ops.clear()

            def body(e, _ops=ops):
                for f in _ops:
                    f(e)
            dec(body)


def build_program():
    nc = bass.Bass("TRN2", target_bir_lowering=False)

    d_x = nc.dram_tensor("x_ext", [TOKE, HID], F32, kind="ExternalInput").ap()
    d_wt = nc.dram_tensor("wt", [NCH, 128, NKT, 128], F32, kind="ExternalInput").ap()
    d_wo = nc.dram_tensor("wo", [64, 128, HID], F32, kind="ExternalInput").ap()
    d_convw = nc.dram_tensor("convw", [128, NXBC * KCONV], F32, kind="ExternalInput").ap()
    d_convb = nc.dram_tensor("convb", [128, NXBC], F32, kind="ExternalInput").ap()
    d_dtb = nc.dram_tensor("dtb", [128, 1], F32, kind="ExternalInput").ap()
    d_alog = nc.dram_tensor("alog", [128, 1], F32, kind="ExternalInput").ap()
    d_dexp = nc.dram_tensor("dexp", [128, NCC], F32, kind="ExternalInput").ap()
    d_normw = nc.dram_tensor("normw", [128, NCC], F32, kind="ExternalInput").ap()
    d_rw = nc.dram_tensor("routew", [NKT, 128, 8], F32, kind="ExternalInput").ap()
    d_at = nc.dram_tensor("a_t", [NKT, 128, 128], F32, kind="ExternalInput").ap()
    d_bwt = nc.dram_tensor("bw_t", [128, HID], F32, kind="ExternalInput").ap()
    d_identb = nc.dram_tensor("identb", [128, 128], BF16, kind="ExternalInput").ap()
    d_identf = nc.dram_tensor("identf", [128, 128], F32, kind="ExternalInput").ap()
    d_trilb = nc.dram_tensor("trilb", [128, 128], BF16, kind="ExternalInput").ap()
    d_mask01 = nc.dram_tensor("mask01", [128, 128], F32, kind="ExternalInput").ap()
    d_amask = nc.dram_tensor("amask", [128, 4], F32, kind="ExternalInput").ap()
    d_out = nc.dram_tensor("out", [TOK, HID], F32, kind="ExternalOutput").ap()

    d_xbc = nc.dram_tensor("s_xbc", [NXBC, 128, TOKE], BF16, kind="Internal").ap()
    d_gate = nc.dram_tensor("s_gate", [NCHK, 128, NCC, CS], BF16, kind="Internal").ap()
    d_bc = nc.dram_tensor("s_bc", [NCHK, 128, 16, CS], BF16, kind="Internal").ap()
    d_wob = nc.dram_tensor("s_wob", [64, 128, HID], BF16, kind="Internal").ap()
    d_sc = nc.dram_tensor("s_sc", [NCHK, 128, INTER], BF16, kind="Internal").ap()
    d_ccin = nc.dram_tensor("s_ccin", [129, INTER], BF16, kind="Internal").ap()
    d_ccout = nc.dram_tensor("s_ccout", [4 * 129, INTER], BF16, kind="Internal").ap()

    top = ExitStack()
    with top:
        def sb(name, shape, dt, st=top):
            return st.enter_context(nc.sbuf_tensor(name, shape, dt))

        def psum(name, shape, dt, st=top):
            return st.enter_context(nc.psum_tensor(name, shape, dt))

        identb = sb("c_identb", [128, 128], BF16)
        identf = sb("c_identf", [128, 128], F32)
        trilb = sb("c_trilb", [128, 128], BF16)
        mask01 = sb("c_mask01", [128, 128], F32)
        convw = sb("c_convw", [128, NXBC * KCONV], F32)
        convb = sb("c_convb", [128, NXBC], F32)
        dtbias = sb("c_dtb", [128, 1], F32)
        negA = sb("c_negA", [128, 1], F32)
        dexp = sb("c_dexp", [128, NCC], F32)
        normw = sb("c_normw", [128, NCC], F32)
        amask = sb("c_amask", [128, 4], F32)
        onesb = sb("c_onesb", [1, 128], BF16)
        onescolb = sb("c_onescolb", [128, 1], BF16)
        lneps = sb("c_lneps", [128, 1], F32)
        rmseps = sb("c_rmseps", [1, 1], F32)
        onecol = sb("c_onecol", [128, 1], F32)

        whfm = sb("w_whfm", [128, TOK], BF16)

        dtf = sb("v_dtf", [128, TOKE], F32)
        dabf = sb("v_dabf", [128, TOK], BF16)
        lndt = sb("v_lndt", [128, TOK], F32)
        cumfm = sb("v_cumfm", [128, TOK], F32)
        cdec = sb("v_cdec", [128, NCHK], F32)
        dfac = sb("v_dfac", [128, NCHK], F32)
        dloccol = sb("v_dloc", [128, 1], F32)
        t8 = sb("v_t8", [128, NCHK], F32)
        gamcol = sb("v_gamcol", [128, 1], F32)
        rowb = sb("v_rowb", [1, 128], BF16)
        bcb = sb("v_bcb", [128, 128], BF16)
        prevb = sb("v_prev", [128, INTER], BF16)
        zrow = sb("v_zrow", [1, 224], F32)

        sems = ExitStack()
        top.enter_context(sems)
        S = {k: Sem(nc, sems, "sm_" + k) for k in
             ["pre", "p0x0", "p0x1", "ffn", "wb0", "wb1", "woi0", "woi1",
              "woo0", "woo1", "stp0", "stp1", "xin0", "xin1", "bcst",
              "bcg0", "bcg1", "scs", "cci", "cc", "gam", "sjp0", "sjp1",
              "gld", "cbf0", "cbf1", "scl", "bwl", "wos0", "wos1",
              "xr0", "xr1", "out0", "out1", "pe", "act", "dve"]}

        ENGS = {"sync": Eng("sync"), "gpsimd": Eng("gpsimd"),
                "scalar": Eng("scalar", auto=S["act"]),
                "vector": Eng("vector", auto=S["dve"]),
                "tensor": Eng("tensor")}

        def engines():
            return ENGS

        cb_state = {"prow_rd": None, "bcb_rd": None, "prow": None, "pbc": None}

        def colbcast(t, s, col_ap, in_tok=None):
            prow_ap = cb_state["prow"]
            pbc_ap = cb_state["pbc"]
            t.wait(in_tok)
            if cb_state["prow_rd"] is not None:
                t.wait(cb_state["prow_rd"])
            tk1 = t.run(lambda e, a=col_ap, o=prow_ap: e.transpose(o, a, identf[:]),
                        S["pe"])
            s.wait(tk1)
            if cb_state["bcb_rd"] is not None:
                s.wait(cb_state["bcb_rd"])
            tk2 = s.run(lambda e, i=prow_ap: e.copy(rowb[:], i), S["act"])
            cb_state["prow_rd"] = tk2
            t.wait(tk2)
            tk3 = t.run(lambda e, o=pbc_ap: e.matmul(o, onesb[:], rowb[:],
                                                     start=True, stop=True), S["pe"])
            s.wait(tk3)
            tk4 = s.run(lambda e, i=pbc_ap: e.copy(bcb[:], i), S["act"])
            t.wait(tk4)
            return tk4

        E = engines()
        g, sy, v, s, t = E["gpsimd"], E["sync"], E["vector"], E["scalar"], E["tensor"]
        tok = None
        for tsb, src in [(identb, d_identb), (identf, d_identf), (trilb, d_trilb),
                         (mask01, d_mask01), (convw, d_convw), (convb, d_convb),
                         (dtbias, d_dtb), (dexp, d_dexp), (normw, d_normw),
                         (amask, d_amask), (negA, d_alog)]:
            tok = g.run(lambda e, a=tsb, b=src: e.dma_start(a[:], b), S["pre"], 16)
        g.wait(tok)
        v.wait(tok)
        v.run(lambda e: e.memset(onesb[:], 1.0))
        v.run(lambda e: e.memset(onescolb[:], 1.0))
        v.run(lambda e: e.memset(lneps[:], EPS))
        v.run(lambda e: e.memset(rmseps[:], EPS))
        v.run(lambda e: e.memset(onecol[:], 1.0))
        v.run(lambda e: e.memset(prevb[:], 0.0))
        v.run(lambda e: e.memset(zrow[:], 0.0))
        s.wait(tok)
        s.run(lambda e: e.activation(negA[:], negA[:], AF.Exp))
        s.run(lambda e: e.mul(negA[:], negA[:], -1.0))
        emit_block(nc, E.values())

        p01 = ExitStack()
        xnf = sb("a_xnf", [128, NKT * TOKE], BF16, p01)
        with ExitStack() as ph:
            sx = [sb(f"a_sx{i}", [128, HID], F32, ph) for i in range(2)]
            xnts = [sb(f"a_xnts{i}", [128, HID], BF16, ph) for i in range(2)]
            bnst = sb("a_bnst", [128, 48], F32, ph)
            mcol = sb("a_mcol", [128, 1], F32, ph)
            vcol = sb("a_vcol", [128, 1], F32, ph)
            rcol = sb("a_rcol", [128, 1], F32, ph)
            m16 = sb("a_m16", [128, 16], F32, ph)
            ptr = [psum(f"p0_tr{i}", [128, 512], BF16, ph) for i in range(2)]

            E = engines()
            g, sy, v, s, t = E["gpsimd"], E["sync"], E["vector"], E["scalar"], E["tensor"]
            rows = [128] * 8 + [3]
            tok_xn = [None] * 9
            tok_trl = [None] * 9
            tok_evac = []

            def stat(r, idx):
                return bnst[0:r, :].rearrange("p (a b) -> p a b", b=3)[:, :, idx]

            for i in range(9):
                r = rows[i]
                if i >= 2:
                    g.wait(tok_xn[i - 2])
                tld = g.run(lambda e, i=i, r=r: e.dma_start(
                    sx[i % 2][0:r, :], d_x[128 * i:128 * i + r, :]),
                    S["p0x%d" % (i % 2)], 16)
                v.wait(tld)
                for s8 in range(8):
                    v.run(lambda e, i=i, r=r, s8=s8: e.bn_stats(
                        bnst[0:r, 6 * s8:6 * s8 + 6],
                        sx[i % 2][0:r, 512 * s8:512 * (s8 + 1)]))
                v.run(lambda e, r=r: e.reduce_sum(mcol[0:r, :], stat(r, 1), axis=AXX))
                v.run(lambda e, r=r: e.tensor_scalar_mul(mcol[0:r, :], mcol[0:r, :], 1.0 / 16.0))
                v.run(lambda e, r=r: e.reduce_sum(vcol[0:r, :], stat(r, 2), axis=AXX))
                v.run(lambda e, r=r: e.tensor_scalar_sub(m16[0:r, :], stat(r, 1), mcol[0:r, :]))
                v.run(lambda e, r=r: e.tensor_mul(m16[0:r, :], m16[0:r, :], m16[0:r, :]))
                v.run(lambda e, r=r: e.reduce_sum(rcol[0:r, :], m16[0:r, :], axis=AXX))
                tv = v.run(lambda e, r=r: e.scalar_tensor_tensor(
                    vcol[0:r, :], rcol[0:r, :], 256.0, vcol[0:r, :],
                    op0=ALU.mult, op1=ALU.add), S["dve"])
                s.wait(tv)
                s.run(lambda e, r=r: e.activation(rcol[0:r, :], vcol[0:r, :], AF.Ln,
                                                  bias=lneps[0:r, :], scale=1.0 / 4096.0))
                ta = s.run(lambda e, r=r: e.activation(rcol[0:r, :], rcol[0:r, :],
                                                       AF.Exp, scale=-0.5), S["act"])
                v.wait(ta)
                if i >= 2:
                    v.wait(tok_trl[i - 2])
                tok_xn[i] = v.run(lambda e, i=i, r=r: e.tensor_scalar(
                    xnts[i % 2][0:r, :], sx[i % 2][0:r, :], mcol[0:r, :], rcol[0:r, :],
                    op0=ALU.subtract, op1=ALU.mult), S["dve"])
                t.wait(tok_xn[i])
                for kb in range(8):
                    gi = 8 * i + kb
                    if gi >= 2:
                        t.wait(tok_evac[gi - 2])
                    tk = None
                    for k4 in range(4):
                        k = 4 * kb + k4
                        tk = t.run(lambda e, i=i, r=r, kb=kb, k4=k4, k=k: e.transpose(
                            ptr[kb % 2][0:128, 128 * k4:128 * k4 + r],
                            xnts[i % 2][0:r, 128 * k:128 * (k + 1)], identb[0:r, 0:r]),
                            S["pe"] if k4 == 3 else None)
                    tok_trl[i] = tk
                    s.wait(tk)
                    te = None
                    for k4 in range(4):
                        k = 4 * kb + k4
                        te = s.run(lambda e, i=i, r=r, kb=kb, k4=k4, k=k: e.copy(
                            xnf[0:128, k * TOKE + 128 * i: k * TOKE + 128 * i + r],
                            ptr[kb % 2][0:128, 128 * k4:128 * k4 + r]),
                            S["act"] if k4 == 3 else None)
                    tok_evac.append(te)
            emit_block(nc, E.values())

        with ExitStack() as ph:
            wb = [sb(f"a_wb{i}", [128, NKT * 128], BF16, ph) for i in range(2)]
            pstage = [sb(f"a_pq{i}", [128, TOKE], BF16, ph) for i in range(2)]
            wob = [sb(f"a_wob{i}", [128, HID], BF16, ph) for i in range(2)]
            ffnw = sb("a_ffnw", [128, NKT * 136], BF16, ph)
            rwl = sb("a_rwl", [128, 8], F32, ph)
            maxc = sb("a_maxc", [128, 1], F32, ph)
            rweq = sb("a_rweq", [128, 8], F32, ph)
            whtk = sb("a_whtk", [128, 128], BF16, ph)
            pmm = [psum(f"p1_mm{i}{j}", [128, TT1[j]], F32, ph)
                   for i in range(2) for j in range(3)]
            pf = psum("p1_f", [128, 512], F32, ph)
            pfl = pf[:, 0:8]
            pfh = pf[:, 128:256]
            pwh = psum("p1_wh", [128, 128], BF16, ph)

            E = engines()
            g, sy, v, s, t = E["gpsimd"], E["sync"], E["vector"], E["scalar"], E["tensor"]
            t.wait(S["act"].cur())
            g.wait(S["act"].cur())
            g.wait(S["dve"].cur())
            g.wait(S["pe"].cur())

            g.run(lambda e: e.dma_start(
                ffnw[:].rearrange("p (k f) -> p k f", k=NKT)[:, :, 0:128],
                d_at.rearrange("k d f -> d k f")), S["ffn"], 16)
            tff = g.run(lambda e: e.dma_start(
                ffnw[:].rearrange("p (k f) -> p k f", k=NKT)[:, :, 128:136],
                d_rw.rearrange("k d f -> d k f")), S["ffn"], 16)

            wo_sched = {}
            for i in range(64):
                wo_sched.setdefault(min(NCH - 1, int(i * NCH / 64.0)), []).append(i)
            tok_woout = {}
            tok_grp = {}
            tok_ev = {}
            tok_sp = {}
            for c in range(NCH):
                if c >= 2:
                    g.wait(tok_grp[(c - 2, 2)])
                twl = g.run(lambda e, c=c: e.dma_start(
                    wb[c % 2][:].rearrange("p (k f) -> p k f", k=NKT),
                    d_wt[c]), S["wb%d" % (c % 2)], 16)
                for i in wo_sched.get(c, []):
                    if i >= 2:
                        g.wait(tok_woout[i - 2])
                    twi = g.run(lambda e, i=i: e.dma_start(
                        wob[i % 2][:], d_wo[i]), S["woi%d" % (i % 2)], 16)
                    sy.wait(twi)
                    tok_woout[i] = sy.run(lambda e, i=i: e.dma_start(
                        d_wob[i], wob[i % 2][:]), S["woo%d" % (i % 2)], 16)
                t.wait(twl)
                for j in range(3):
                    t0 = sum(TT1[:j])
                    if c >= 2:
                        t.wait(tok_ev[(c - 2, j)])
                    tk = None
                    for k in range(NKT):
                        tk = t.run(lambda e, c=c, j=j, k=k, t0=t0: e.matmul(
                            pmm[(c % 2) * 3 + j][:],
                            wb[c % 2][:, 128 * k:128 * (k + 1)],
                            xnf[:, k * TOKE + t0: k * TOKE + t0 + TT1[j]],
                            start=(k == 0), stop=(k == NKT - 1)),
                            S["pe"] if k == NKT - 1 else None)
                    tok_grp[(c, j)] = tk
                for j in range(3):
                    t0 = sum(TT1[:j])
                    s.wait(tok_grp[(c, j)])
                    if c >= 2 and (c - 2, 2) in tok_sp:
                        s.wait(tok_sp[(c - 2, 2)])
                    if c < NCC:
                        te = s.run(lambda e, c=c, j=j, t0=t0: e.activation(
                            pstage[c % 2][:, t0:t0 + TT1[j]],
                            pmm[(c % 2) * 3 + j][:], AF.Silu), S["act"])
                    elif c < NCC + NXBC:
                        te = s.run(lambda e, c=c, j=j, t0=t0: e.copy(
                            pstage[c % 2][:, t0:t0 + TT1[j]],
                            pmm[(c % 2) * 3 + j][:]), S["act"])
                    else:
                        te = s.run(lambda e, c=c, j=j, t0=t0: e.copy(
                            dtf[:, t0:t0 + TT1[j]],
                            pmm[(c % 2) * 3 + j][:]), S["act"])
                    tok_ev[(c, j)] = te
                if c < NCC + NXBC:
                    for j in range(3):
                        t0 = sum(TT1[:j])
                        sy.wait(tok_ev[(c, j)])
                        if c < NCC:
                            lo = max(t0, 3)
                            hi = t0 + TT1[j]
                            tsp = None
                            while lo < hi:
                                ch = (lo - 3) // CS
                                nxt = min(hi, 3 + (ch + 1) * CS)
                                tsp = sy.run(lambda e, c=c, lo=lo, nxt=nxt, ch=ch:
                                             e.dma_start(
                                    d_gate[ch, :, c,
                                           (lo - 3) - ch * CS:(nxt - 3) - ch * CS],
                                    pstage[c % 2][:, lo:nxt]),
                                    S["stp%d" % (c % 2)], 16)
                                lo = nxt
                        else:
                            tsp = sy.run(lambda e, c=c, j=j, t0=t0: e.dma_start(
                                d_xbc[c - NCC, :, t0:t0 + TT1[j]],
                                pstage[c % 2][:, t0:t0 + TT1[j]]),
                                S["stp%d" % (c % 2)], 16)
                        tok_sp[(c, j)] = tsp

            t.wait(tff)
            tw_prev = None
            tev_prev = None
            for tt in range(8):
                if tw_prev is not None:
                    t.wait(tw_prev)
                tkh = None
                for k in range(NKT):
                    t.run(lambda e, tt=tt, k=k: e.matmul(
                        pfl,
                        xnf[:, k * TOKE + 3 + 128 * tt: k * TOKE + 3 + 128 * (tt + 1)],
                        ffnw[:, k * 136 + 128: k * 136 + 136],
                        start=(k == 0), stop=(k == NKT - 1), skip_group_check=True))
                    tkh = t.run(lambda e, tt=tt, k=k: e.matmul(
                        pfh,
                        xnf[:, k * TOKE + 3 + 128 * tt: k * TOKE + 3 + 128 * (tt + 1)],
                        ffnw[:, k * 136: k * 136 + 128],
                        start=(k == 0), stop=(k == NKT - 1), skip_group_check=True),
                        S["pe"] if k == NKT - 1 else None)
                v.wait(tkh)
                v.run(lambda e: e.tensor_copy(rwl[:], pfl))
                v.run(lambda e: e.reduce_max(maxc[:], rwl[:], axis=AXX))
                v.run(lambda e: e.tensor_single_scalar(rweq[:], rwl[:], maxc[:],
                                                       op=ALU.is_equal))
                v.run(lambda e: e.tensor_single_scalar(rweq[:], rweq[:], maxc[:],
                                                       op=ALU.mult))
                tw_prev = v.run(lambda e: e.tensor_tensor(
                    whtk[:].rearrange("p (a b) -> p a b", a=8),
                    pfh.rearrange("p (a b) -> p a b", a=8),
                    rweq[:].unsqueeze(2).broadcast_to([128, 8, 16]),
                    op=ALU.mult), S["dve"])
                t.wait(tw_prev)
                if tev_prev is not None:
                    t.wait(tev_prev)
                ttr = t.run(lambda e: e.transpose(pwh[:], whtk[:], identb[:]), S["pe"])
                s.wait(ttr)
                tev_prev = s.run(lambda e, tt=tt: e.copy(
                    whfm[:, 128 * tt:128 * (tt + 1)], pwh[:]), S["act"])

            for k in ["wb0", "wb1", "woi0", "woi1", "ffn"]:
                g.wait(S[k].cur())
            for k in ["woo0", "woo1", "stp0", "stp1"]:
                sy.wait(S[k].cur())
            emit_block(nc, E.values())
        p01.close()

        p23 = ExitStack()
        region = sb("a_region", [128, NCC * TOK], BF16, p23)

        def regv(cc, lo, n):
            return region[:, cc * TOK + lo: cc * TOK + lo + n]

        def regchunk(c):
            return region[:].rearrange("p (a b) -> p a b", a=NCC)[:, :, c * CS:(c + 1) * CS]

        tok_cv_out = [None] * NXBC
        with ExitStack() as ph:
            xin = [sb(f"a_xin{i}", [128, TOKE], BF16, ph) for i in range(2)]
            cvt = sb("a_cvt", [128, TOK], BF16, ph)
            bstg = sb("a_bstg", [128, TOK], BF16, ph)

            E = engines()
            g, sy, v, s, t = E["gpsimd"], E["sync"], E["vector"], E["scalar"], E["tensor"]
            for k in ["stp0", "stp1", "woo0", "woo1"]:
                g.wait(S[k].cur())
            g.wait(S["act"].cur())
            g.wait(S["dve"].cur())
            g.wait(S["pe"].cur())
            tok_cv_dve = [None] * NXBC
            tok_bc_sp = None
            for cc in range(NXBC):
                if cc >= 2:
                    g.wait(tok_cv_dve[cc - 2])
                tld = g.run(lambda e, cc=cc: e.dma_start(
                    xin[cc % 2][:], d_xbc[cc]), S["xin%d" % (cc % 2)], 16)
                v.wait(tld)
                if cc > 0:
                    v.wait(tok_cv_out[cc - 1])
                v.run(lambda e, cc=cc: e.tensor_single_scalar(
                    cvt[:], xin[cc % 2][:, 0:TOK], convw[:, 4 * cc:4 * cc + 1],
                    op=ALU.mult))
                tk = None
                for k in range(1, 4):
                    tk = v.run(lambda e, cc=cc, k=k: e.scalar_tensor_tensor(
                        cvt[:], xin[cc % 2][:, k:k + TOK],
                        convw[:, 4 * cc + k:4 * cc + k + 1],
                        cvt[:], op0=ALU.mult, op1=ALU.add),
                        S["dve"] if k == 3 else None)
                tok_cv_dve[cc] = tk
                s.wait(tk)
                if cc < NCC:
                    tok_cv_out[cc] = s.run(lambda e, cc=cc: e.activation(
                        regv(cc, 0, TOK), cvt[:], AF.Silu,
                        bias=convb[:, cc:cc + 1]), S["act"])
                else:
                    if tok_bc_sp is not None:
                        s.wait(tok_bc_sp)
                    tso = s.run(lambda e, cc=cc: e.activation(
                        bstg[:], cvt[:], AF.Silu, bias=convb[:, cc:cc + 1]), S["act"])
                    sy.wait(tso)
                    for ch in range(NCHK):
                        tok_bc_sp = sy.run(lambda e, cc=cc, ch=ch: e.dma_start(
                            d_bc[ch, :, cc - NCC, :],
                            bstg[:, ch * CS:(ch + 1) * CS]), S["bcst"], 16)
                    tok_cv_out[cc] = tok_bc_sp
            g.wait(S["xin0"].cur())
            g.wait(S["xin1"].cur())
            sy.wait(S["bcst"].cur())
            emit_block(nc, E.values())

        with ExitStack() as ph:
            sloc = sb("a_sloc", [128, INTER], BF16, ph)
            xtok = sb("a_xtok", [128, 1024], BF16, ph)
            xsg = sb("a_xsg", [128, 1024], BF16, ph)
            btok = sb("a_btok", [128, 128], BF16, ph)
            bcg = [sb(f"a_bcg{i}", [128, 2048], BF16, ph) for i in range(2)]
            dfbc = sb("a_dfbc", [128, 128], BF16, ph)
            stmp = sb("a_stmp", [128, 1024], BF16, ph)
            wtokb = sb("a_wtokb", [128, 128], BF16, ph)
            dlastc = sb("a_dlastc", [128, 128], F32, ph)
            wfmc = sb("a_wfmc", [128, 128], F32, ph)
            datok = sb("a_datok", [128, 128], BF16, ph)
            cumtk = sb("a_cumtk", [128, 128], F32, ph)
            biasmf = sb("a_biasmf", [128, 128], F32, ph)
            biastk = sb("a_biastk", [128, 128], F32, ph)
            cumrep = sb("a_cumrep", [128, 512], F32, ph)
            e4one = sb("a_e4", [128, 512], F32, ph)
            e4 = [e4one, e4one]
            sc4 = [sb(f"a_sc4{i}", [128, 512], BF16, ph) for i in range(2)]
            grammone = sb("a_gramm", [128, 128], F32, ph)
            gramm = [grammone, grammone]
            scst = sb("a_scst", [128, 1024], BF16, ph)
            sjp = bcg

            pmisc = psum("p2_misc", [128, 512], F32, ph)
            pcm0 = pmisc[:, 0:128]
            pgram0 = pmisc[:, 128:256]
            cb_state["pbc"] = pmisc[:, 256:384]
            cb_state["prow"] = pmisc[0:1, 384:512]
            ptrb = [psum(f"p2_trb{i}", [128, 512], BF16, ph) for i in range(2)]
            pcb = [psum(f"p2_cb{i}", [128, 512], F32, ph) for i in range(2)]
            pst0 = psum("p2_st", [128, 512], F32, ph)
            py = [psum(f"p2_y{i}", [128, 256], F32, ph) for i in range(2)]

            E = engines()
            g, sy, v, s, t = E["gpsimd"], E["sync"], E["vector"], E["scalar"], E["tensor"]
            t.wait(S["act"].cur())
            t.wait(S["dve"].cur())
            g.wait(S["act"].cur())
            g.wait(S["dve"].cur())
            g.wait(S["pe"].cur())

            s.run(lambda e: e.add(dtf[:], dtf[:], dtbias[:]))
            tsp0 = s.run(lambda e: e.activation(lndt[:], dtf[:, 3:TOKE], AF.Abs), S["act"])
            s.run(lambda e: e.activation(lndt[:], lndt[:], AF.Exp, scale=-1.0))
            tsp1 = s.run(lambda e: e.activation(lndt[:], lndt[:], AF.Ln,
                                                bias=onecol[:]), S["act"])
            v.wait(tsp0)
            v.run(lambda e: e.tensor_scalar_max(cumfm[:], dtf[:, 3:TOKE], 0.0))
            v.wait(tsp1)
            tdt = v.run(lambda e: e.tensor_add(dtf[:, 3:TOKE], cumfm[:], lndt[:]), S["dve"])
            v.run(lambda e: e.tensor_single_scalar(dabf[:], dtf[:, 3:TOKE], negA[:],
                                                   op=ALU.mult))
            tda = v.run(lambda e: e.tensor_copy(dabf[0:1, 0:1], dabf[0:1, 0:1]), S["dve"])
            s.wait(tdt)
            tlnd = s.run(lambda e: e.activation(lndt[:], dtf[:, 3:TOKE], AF.Ln), S["act"])

            tok_cum = [None] * NCHK
            for c in range(NCHK):
                t.wait(tda)
                ttr = t.run(lambda e, c=c: e.transpose(
                    ptrb[0][:, 0:128], dabf[:, c * CS:(c + 1) * CS], identb[:]), S["pe"])
                s.wait(ttr)
                tevd = s.run(lambda e: e.copy(datok[:], ptrb[0][:, 0:128]), S["act"])
                t.wait(tevd)
                if c > 0:
                    t.wait(tok_cum[c - 1])
                tmm = t.run(lambda e, c=c: e.matmul(
                    pcm0, datok[:], trilb[:], start=True, stop=True,
                    skip_group_check=True), S["pe"])
                s.wait(tmm)
                tok_cum[c] = s.run(lambda e, c=c: e.copy(
                    cumfm[:, c * CS:(c + 1) * CS], pcm0), S["act"])
            s.wait(tok_cum[NCHK - 1])
            s.run(lambda e: e.activation(
                cdec[:], cumfm[:].rearrange("p (c i) -> p c i", c=NCHK)[:, :, 127],
                AF.Exp))
            v.wait(tok_cum[NCHK - 1])
            v.run(lambda e: e.memset(t8[:, NCHK - 1:NCHK], 0.0))
            for c in range(NCHK - 2, -1, -1):
                v.run(lambda e, c=c: e.tensor_add(
                    t8[:, c:c + 1], t8[:, c + 1:c + 2],
                    cumfm[:, (c + 1) * CS + 127:(c + 1) * CS + 128]))
            tvs = v.run(lambda e: e.tensor_add(
                dloccol[:], t8[:, 0:1], cumfm[:, 127:128]), S["dve"])
            v.run(lambda e: e.memset(sloc[:], 0.0))
            s.wait(tvs)
            s.run(lambda e: e.activation(dfac[:], t8[:], AF.Exp))
            tdfac = s.run(lambda e: e.activation(dloccol[:], dloccol[:], AF.Exp), S["act"])

            tok_bcld = [None] * NCHK
            tok_bcg_rd = {}
            ptrb_rd = {0: tevd, 1: None}
            py_tok = {}
            sc_tok = {}
            e4_exp_tok = None
            tok_biastk_rd = None
            pcb_tok = {}
            tok_cumrep_rd = None
            tok_cumtk_rd = None
            tok_wtok_rd = None
            tok_dfbc_rd = None
            tok_scst_dve = None
            tok_scst_sp = None
            tok_btok_rd = None
            tok_gram_rd = None
            tok_xtok_rd = []
            for c in range(NCHK):
                g.wait(tok_cv_out[NXBC - 1])
                if c >= 2:
                    g.wait(tok_bcg_rd[c - 2])
                tok_bcld[c] = g.run(lambda e, c=c: e.dma_start(
                    bcg[c % 2][:].rearrange("p (a b) -> p a b", a=16),
                    d_bc[c]), S["bcg%d" % (c % 2)], 16)

                tdfbcv = colbcast(t, s, dfac[:, c:c + 1], in_tok=tdfac)
                v.wait(tdfbcv)
                if tok_dfbc_rd is not None:
                    v.wait(tok_dfbc_rd)
                tvcp = v.run(lambda e: e.tensor_copy(dfbc[:], bcb[:]), S["dve"])
                cb_state["bcb_rd"] = tvcp

                if tok_wtok_rd is not None:
                    s.wait(tok_wtok_rd)
                if c == 0:
                    t.wait(tok_cum[NCHK - 1])
                else:
                    t.wait(tok_pcm0_rd)
                tdl = s.run(lambda e, c=c: e.activation(
                    dlastc[:], cumfm[:, c * CS:(c + 1) * CS], AF.Exp, scale=-1.0,
                    bias=cumfm[:, c * CS + 127:c * CS + 128]), S["act"])
                v.wait(tdl)
                twf = v.run(lambda e, c=c: e.tensor_mul(
                    wfmc[:], dlastc[:], dtf[:, 3 + c * CS:3 + (c + 1) * CS]), S["dve"])
                t.wait(twf)
                tw1 = t.run(lambda e: e.transpose(pcm0, wfmc[:], identf[:]), S["pe"])
                s.wait(tw1)
                tw2 = s.run(lambda e: e.copy(wtokb[:], pcm0), S["act"])

                v.wait(tok_cum[c])
                v.wait(tlnd)
                tbm = v.run(lambda e, c=c: e.tensor_sub(
                    biasmf[:], lndt[:, c * CS:(c + 1) * CS],
                    cumfm[:, c * CS:(c + 1) * CS]), S["dve"])
                t.wait(tbm)
                t.wait(tw2)
                tb1 = t.run(lambda e: e.transpose(pcm0, biasmf[:], identf[:]), S["pe"])
                s.wait(tb1)
                if tok_biastk_rd is not None:
                    s.wait(tok_biastk_rd)
                tb2 = s.run(lambda e: e.copy(biastk[:], pcm0), S["act"])
                t.wait(tb2)
                tc1 = t.run(lambda e, c=c: e.transpose(
                    pcm0, cumfm[:, c * CS:(c + 1) * CS], identf[:]), S["pe"])
                s.wait(tc1)
                if tok_cumtk_rd is not None:
                    s.wait(tok_cumtk_rd)
                tc2 = s.run(lambda e, c=c: e.copy(cumtk[:], pcm0), S["act"])
                tok_pcm0_rd = tc2

                for gg in range(8):
                    txe = None
                    first_evac = True
                    for q in range(8):
                        cc = 8 * gg + q
                        t.wait(tok_cv_out[cc])
                        pq = q // 4
                        if ptrb_rd[pq] is not None:
                            t.wait(ptrb_rd[pq])
                            ptrb_rd[pq] = None
                        tk = t.run(lambda e, cc=cc, q=q, c=c: e.transpose(
                            ptrb[q // 4][:, 128 * (q % 4):128 * (q % 4 + 1)],
                            regv(cc, c * CS, CS), identb[:]),
                            S["pe"] if q % 4 == 3 else None)
                        if q % 4 == 3:
                            s.wait(tk)
                            if first_evac:
                                for tkx in tok_xtok_rd:
                                    s.wait(tkx)
                                tok_xtok_rd = []
                                first_evac = False
                            txe = s.run(lambda e, q=q: e.copy(
                                xtok[:, 512 * (q // 4):512 * (q // 4) + 512],
                                ptrb[q // 4][:]), S["act"])
                            ptrb_rd[q // 4] = txe
                    v.wait(txe)
                    v.wait(tw2)
                    if tok_scst_dve is not None:
                        v.wait(tok_scst_dve)
                    txs = v.run(lambda e, gg=gg: e.tensor_tensor(
                        xsg[:].rearrange("p (h q) -> p h q", h=16),
                        xtok[:].rearrange("p (h q) -> p h q", h=16),
                        wtokb[:, 16 * gg:16 * (gg + 1)].unsqueeze(2)
                        .broadcast_to([128, 16, 64]), op=ALU.mult), S["dve"])
                    tok_wtok_rd = txs
                    tok_xtok_rd.append(txs)
                    t.wait(tok_bcld[c])
                    if ptrb_rd[0] is not None:
                        t.wait(ptrb_rd[0])
                        ptrb_rd[0] = None
                    if tok_btok_rd is not None:
                        t.wait(tok_btok_rd)
                    tbt = t.run(lambda e, gg=gg, c=c: e.transpose(
                        ptrb[0][:, 0:128], bcg[c % 2][:, gg * 128:(gg + 1) * 128],
                        identb[:]), S["pe"])
                    s.wait(tbt)
                    tbte = s.run(lambda e: e.copy(btok[:], ptrb[0][:, 0:128]), S["act"])
                    ptrb_rd[0] = tbte
                    if tok_gram_rd is not None:
                        t.wait(tok_gram_rd)
                    tgr = t.run(lambda e, gg=gg, c=c: e.matmul(
                        pgram0, bcg[c % 2][:, gg * 128:(gg + 1) * 128],
                        bcg[c % 2][:, 1024 + gg * 128:1024 + (gg + 1) * 128],
                        start=True, stop=True, skip_group_check=True), S["pe"])
                    tok_bcg_rd[c] = tgr
                    v.wait(tgr)
                    tgm = v.run(lambda e, gg=gg: e.tensor_mul(
                        gramm[gg % 2][:], pgram0, mask01[:]), S["dve"])
                    tok_gram_rd = tgm
                    t.wait(txs)
                    t.wait(tbte)
                    tse = None
                    for half in range(2):
                        tst = None
                        for hh in range(8):
                            hloc = half * 8 + hh
                            tst = t.run(lambda e, half=half, hh=hh, hloc=hloc: e.matmul(
                                pst0[:, 64 * hh:64 * (hh + 1)], btok[:],
                                xsg[:, 64 * hloc:64 * (hloc + 1)],
                                start=True, stop=True),
                                S["pe"] if hh == 7 else None)
                        s.wait(tst)
                        if tok_scst_sp is not None:
                            s.wait(tok_scst_sp)
                        if tok_scst_dve is not None:
                            s.wait(tok_scst_dve)
                        tse = s.run(lambda e, half=half: e.copy(
                            scst[:, 512 * half:512 * (half + 1)], pst0[:]),
                            S["act"])
                        t.wait(tse)
                    tok_btok_rd = tst
                    sy.wait(tse)
                    tok_scst_sp = sy.run(lambda e, c=c, gg=gg: e.dma_start(
                        d_sc[c, :, 1024 * gg:1024 * (gg + 1)], scst[:]), S["scs"], 16)
                    v.wait(tse)
                    v.wait(tvcp)
                    v.run(lambda e, gg=gg: e.tensor_tensor(
                        stmp[:].rearrange("p (h q) -> p h q", h=16),
                        scst[:].rearrange("p (h q) -> p h q", h=16),
                        dfbc[:, 16 * gg:16 * (gg + 1)].unsqueeze(2)
                        .broadcast_to([128, 16, 64]), op=ALU.mult))
                    tok_scst_dve = v.run(lambda e, gg=gg: e.tensor_add(
                        sloc[:, 1024 * gg:1024 * (gg + 1)],
                        sloc[:, 1024 * gg:1024 * (gg + 1)], stmp[:]), S["dve"])
                    tok_dfbc_rd = tok_scst_dve

                    for hb in range(4):
                        h0 = 16 * gg + 4 * hb
                        bi = hb % 2
                        v.wait(tc2)
                        if tok_cumrep_rd is not None:
                            v.wait(tok_cumrep_rd)
                        tcr = v.run(lambda e, h0=h0: e.tensor_copy(
                            cumrep[:].rearrange("p (a b) -> p a b", a=4),
                            cumtk[:, h0 % 128:h0 % 128 + 4].unsqueeze(2)
                            .broadcast_to([128, 4, 128])), S["dve"])
                        tok_cumtk_rd = tcr
                        t.wait(tcr)
                        if pcb_tok.get(bi) is not None:
                            t.wait(pcb_tok[bi])
                        tk = None
                        for q in range(4):
                            tk = t.run(lambda e, q=q, bi=bi: e.transpose(
                                pcb[bi][:, 128 * q:128 * (q + 1)],
                                cumrep[:, 128 * q:128 * (q + 1)], identf[:]),
                                S["pe"] if q == 3 else None)
                        tok_cumrep_rd = tk
                        v.wait(tk)
                        v.wait(tb2)
                        if e4_exp_tok is not None:
                            v.wait(e4_exp_tok)
                        tva = None
                        for q in range(4):
                            hh = h0 + q
                            tva = v.run(lambda e, q=q, hh=hh, bi=bi: e.tensor_scalar(
                                e4[bi][:, 128 * q:128 * (q + 1)],
                                pcb[bi][:, 128 * q:128 * (q + 1)],
                                biastk[:, hh % 128:hh % 128 + 1], 30.0,
                                op0=ALU.add, op1=ALU.min),
                                S["dve"] if q == 3 else None)
                        pcb_tok[bi] = tva
                        tok_biastk_rd = tva
                        s.wait(tva)
                        ta = s.run(lambda e, bi=bi: e.activation(
                            e4[bi][:], e4[bi][:], AF.Exp), S["act"])
                        e4_exp_tok = ta
                        v.wait(ta)
                        v.wait(tgm)
                        if sc_tok.get(bi) is not None:
                            v.wait(sc_tok[bi])
                        tsc = v.run(lambda e, bi=bi, gg=gg: e.tensor_tensor(
                            sc4[bi][:].rearrange("p (a b) -> p a b", a=4),
                            e4[bi][:].rearrange("p (a b) -> p a b", a=4),
                            gramm[gg % 2][:].unsqueeze(1).broadcast_to([128, 4, 128]),
                            op=ALU.mult), S["dve"])
                        t.wait(tsc)
                        if py_tok.get(bi) is not None:
                            t.wait(py_tok[bi])
                        tky = None
                        for q in range(4):
                            hh = h0 + q
                            hloc = 4 * hb + q
                            tky = t.run(lambda e, q=q, hh=hh, hloc=hloc, bi=bi, gg=gg:
                                        e.matmul(
                                py[bi][64 * (hh % 2):64 * (hh % 2) + 64,
                                       128 * (q // 2):128 * (q // 2) + 128],
                                xtok[:, 64 * hloc:64 * (hloc + 1)],
                                sc4[bi][:, 128 * q:128 * (q + 1)],
                                start=True, stop=True,
                                tile_position=(0, 64 * (hh % 2))),
                                S["pe"] if q == 3 else None)
                        sc_tok[bi] = tky
                        if hb == 3:
                            tok_xtok_rd.append(tky)
                        v.wait(tky)
                        tyv = None
                        for q2 in range(2):
                            cc = h0 // 2 + q2
                            tyv = v.run(lambda e, cc=cc, q2=q2, bi=bi, c=c:
                                        e.scalar_tensor_tensor(
                                regv(cc, c * CS, CS), regv(cc, c * CS, CS),
                                dexp[:, cc:cc + 1],
                                py[bi][:, 128 * q2:128 * (q2 + 1)],
                                op0=ALU.mult, op1=ALU.add), S["dve"])
                        py_tok[bi] = tyv

            g.wait(S["bcg0"].cur())
            g.wait(S["bcg1"].cur())
            sy.wait(S["scs"].cur())
            emit_block(nc, E.values())

            E = engines()
            g, sy, v, s, t = E["gpsimd"], E["sync"], E["vector"], E["scalar"], E["tensor"]
            g.wait(S["dve"].cur())
            g.wait(S["act"].cur())
            g.run(lambda e: e.dma_start(d_ccin[0:128, :], sloc[:]), S["cci"], 16)
            g.run(lambda e: e.dma_start(
                d_ccin[128:129, 0:128], dloccol[:]), S["cci"], 16)
            t2 = None
            for zi in range(36):
                t2 = g.run(lambda e, zi=zi: e.dma_start(
                    d_ccin[128:129, 128 + 224 * zi:128 + 224 * (zi + 1)],
                    zrow[:]), S["cci"], 16)
            g.wait(t2)
            tcc = g.run(lambda e: e.collective_compute(
                "AllGather", ALU.bypass,
                replica_groups=[[0, 1, 2, 3], [4, 5, 6, 7]],
                ins=[d_ccin.opt()], outs=[d_ccout.opt()]), S["cc"], 1)
            g.wait(tcc)
            tok_stt = {}
            tgb = None
            for j in range(3):
                if tgb is not None:
                    g.wait(tgb)
                tdj = g.run(lambda e, j=j: e.dma_start(
                    gamcol[:], d_ccout[129 * j + 128:129 * j + 129, 0:128]),
                    S["gam"], 16)
                v.wait(tdj)
                v.run(lambda e, j=j: e.tensor_scalar(
                    gamcol[:], gamcol[:], 1.0, amask[:, j:j + 1],
                    op0=ALU.subtract, op1=ALU.mult))
                tgam = v.run(lambda e: e.tensor_scalar_add(gamcol[:], gamcol[:], 1.0),
                             S["dve"])
                tgb = colbcast(t, s, gamcol[:], in_tok=tgam)
                v.wait(tgb)
                tvp = v.run(lambda e: e.tensor_tensor(
                    prevb[:].rearrange("p (h q) -> p h q", h=128),
                    prevb[:].rearrange("p (h q) -> p h q", h=128),
                    bcb[:].unsqueeze(2).broadcast_to([128, 128, 64]),
                    op=ALU.mult), S["dve"])
                cb_state["bcb_rd"] = tvp
                for qq in range(4):
                    if tok_stt.get(qq % 2) is not None:
                        g.wait(tok_stt[qq % 2])
                    tsj = g.run(lambda e, j=j, qq=qq: e.dma_start(
                        sjp[qq % 2][:], d_ccout[129 * j:129 * j + 128,
                                                2048 * qq:2048 * (qq + 1)]),
                        S["sjp%d" % (qq % 2)], 16)
                    v.wait(tsj)
                    tok_stt[qq % 2] = v.run(lambda e, j=j, qq=qq: e.scalar_tensor_tensor(
                        prevb[:, 2048 * qq:2048 * (qq + 1)],
                        sjp[qq % 2][:], amask[:, j:j + 1],
                        prevb[:, 2048 * qq:2048 * (qq + 1)],
                        op0=ALU.mult, op1=ALU.add), S["dve"])
            g.wait(S["gam"].cur())
            g.wait(S["sjp0"].cur())
            g.wait(S["sjp1"].cur())
            emit_block(nc, E.values())

        with ExitStack() as ph:
            gateb = sb("a_gateb", [128, INTER], BF16, ph)
            cbuf = [sb(f"a_cbuf{i}", [128, 1024], BF16, ph) for i in range(2)]
            scbone = sb("a_scb", [128, 4096], BF16, ph)
            scb = [scbone, scbone]
            expck = sb("a_expck", [128, 128], F32, ph)
            y2s = sb("a_y2s", [128, 256], BF16, ph)
            cdbc = sb("a_cdbc", [128, 128], BF16, ph)
            rrow = sb("a_rrow", [1, 1024], F32, ph)
            rrowb = sb("a_rrowb", [1, 1024], BF16, ph)
            rstdb = sb("a_rstdb", [128, 1024], BF16, ph)

            pmisc3 = psum("p3_misc", [128, 512], F32, ph)
            pcm3 = pmisc3[:, 0:128]
            cb_state["pbc"] = pmisc3[:, 128:256]
            cb_state["prow"] = pmisc3[0:1, 384:512]
            cb_state["prow_rd"] = None
            cb_state["bcb_rd"] = None
            pa = [psum(f"p3_a{i}", [128, 256], F32, ph) for i in range(2)]
            pb = [psum(f"p3_b{i}", [128, 128], BF16, ph) for i in range(2)]
            pvar = [psum(f"p3_v{i}", [1, 512], F32, ph) for i in range(2)]
            prr = psum("p3_rr", [128, 512], F32, ph)

            E = engines()
            g, sy, v, s, t = E["gpsimd"], E["sync"], E["vector"], E["scalar"], E["tensor"]
            g.wait(S["scs"].cur())
            g.wait(S["act"].cur())
            g.wait(S["dve"].cur())
            g.wait(S["pe"].cur())
            t.wait(S["act"].cur())
            t.wait(S["dve"].cur())

            tok_gate_used = {}
            tok_c_used = {}
            tok_s_used = {}
            pa_tok = {}
            pb_tok = {}
            y2s_rd = None
            tok_expck_rd = None
            tok_cdbc_rd = None
            prev_trr = None
            prev_trs = None
            for c in range(NCHK):
                if c >= 1:
                    g.wait(tok_gate_used.get(c - 1))
                tok_gld = g.run(lambda e, c=c: e.dma_start(
                    gateb[:].rearrange("p (a b) -> p a b", a=NCC),
                    d_gate[c]), S["gld"], 16)
                if c >= 2:
                    g.wait(tok_c_used.get(c - 2))
                tok_cld = g.run(lambda e, c=c: e.dma_start(
                    cbuf[c % 2][:].rearrange("p (a b) -> p a b", a=8),
                    d_bc[c, :, 8:16, :]), S["cbf%d" % (c % 2)], 16)
                if c >= 1:
                    g.wait(tok_s_used.get(c - 1))
                tok_s0 = g.run(lambda e, c=c: e.dma_start(
                    scb[c % 2][:], d_sc[c, :, 0:4096]), S["scl"], 16)

                if tok_expck_rd is not None:
                    s.wait(tok_expck_rd)
                tc1 = t.run(lambda e, c=c: e.transpose(
                    pcm3, cumfm[:, c * CS:(c + 1) * CS], identf[:]), S["pe"])
                s.wait(tc1)
                tc2 = s.run(lambda e: e.activation(expck[:], pcm3, AF.Exp), S["act"])
                t.wait(tc2)

                tcb = colbcast(t, s, cdec[:, c:c + 1])
                v.wait(tcb)
                if tok_cdbc_rd is not None:
                    v.wait(tok_cdbc_rd)
                tcdbc = v.run(lambda e: e.tensor_copy(cdbc[:], bcb[:]), S["dve"])
                cb_state["bcb_rd"] = tcdbc

                last_mm = None
                for gg in range(8):
                    t.wait(tok_cld)
                    for hb in range(4):
                        bi = hb % 2
                        h0 = 16 * gg + 4 * hb
                        if pa_tok.get(bi) is not None:
                            t.wait(pa_tok[bi])
                        tk = None
                        for q in range(4):
                            hh = h0 + q
                            tk = t.run(lambda e, gg=gg, hh=hh, q=q, bi=bi, c=c: e.matmul(
                                pa[bi][:, 64 * q:64 * (q + 1)],
                                cbuf[c % 2][:, 128 * gg:128 * (gg + 1)],
                                prevb[:, 64 * hh:64 * (hh + 1)],
                                start=True, stop=True),
                                S["pe"] if q == 3 else None)
                        last_mm = tk
                        v.wait(tk)
                        v.wait(tc2)
                        if y2s_rd is not None:
                            v.wait(y2s_rd)
                        tvv = None
                        for q in range(4):
                            hh = h0 + q
                            tvv = v.run(lambda e, q=q, hh=hh, bi=bi: e.tensor_single_scalar(
                                y2s[:, 64 * q:64 * (q + 1)],
                                pa[bi][:, 64 * q:64 * (q + 1)],
                                expck[:, hh % 128:hh % 128 + 1], op=ALU.mult),
                                S["dve"] if q == 3 else None)
                        pa_tok[bi] = tvv
                        tok_expck_rd = tvv
                        t.wait(tvv)
                        tkb_last = None
                        for pr in range(2):
                            if pb_tok.get(pr) is not None:
                                t.wait(pb_tok[pr])
                            tkb = None
                            for q2 in range(2):
                                hh = h0 + 2 * pr + q2
                                tkb = t.run(lambda e, pr=pr, q2=q2: e.transpose(
                                    pb[pr][64 * q2:64 * (q2 + 1), :],
                                    y2s[:, 64 * (2 * pr + q2):64 * (2 * pr + q2 + 1)],
                                    identb[:], tile_position=(0, 64 * q2)),
                                    S["pe"] if q2 == 1 else None)
                            tkb_last = tkb
                            v.wait(tkb)
                            cc = (h0 + 2 * pr) // 2
                            tvb = v.run(lambda e, cc=cc, c=c, pr=pr: e.tensor_add(
                                regv(cc, c * CS, CS), regv(cc, c * CS, CS),
                                pb[pr][:]), S["dve"])
                            pb_tok[pr] = tvb
                        y2s_rd = tkb_last
                tok_c_used[c] = last_mm

                v.wait(tok_gld)
                tg1 = v.run(lambda e, c=c: e.tensor_mul(
                    regchunk(c), regchunk(c),
                    gateb[:].rearrange("p (a b) -> p a b", a=NCC)), S["dve"])
                s.wait(tg1)
                tg2 = s.run(lambda e, c=c: e.activation(
                    gateb[:].rearrange("p (a b) -> p a b", a=NCC),
                    regchunk(c), AF.Square), S["act"])
                t.wait(tg2)
                if prev_trr is not None:
                    t.wait(prev_trr)
                tvm3 = tvm7 = None
                for grp in range(8):
                    tkv = None
                    for qq in range(8):
                        cc = 8 * grp + qq
                        tkv = t.run(lambda e, grp=grp, cc=cc, qq=qq: e.matmul(
                            pvar[grp // 4][0:1, 128 * (grp % 4):128 * (grp % 4) + 128],
                            onescolb[:], gateb[:, cc * 128:(cc + 1) * 128],
                            start=(qq == 0), stop=(qq == 7)),
                            S["pe"] if qq == 7 else None)
                    if grp == 3:
                        tvm3 = tkv
                    if grp == 7:
                        tvm7 = tkv
                tok_gate_used[c] = tvm7
                s.wait(tvm3)
                s.wait(tvm7)
                s.run(lambda e: e.activation(rrow[0:1, 0:512], pvar[0][:], AF.Ln,
                                             bias=rmseps[:], scale=1.0 / 1024.0))
                s.run(lambda e: e.activation(rrow[0:1, 512:1024], pvar[1][:], AF.Ln,
                                             bias=rmseps[:], scale=1.0 / 1024.0))
                trr = s.run(lambda e: e.activation(rrow[:], rrow[:], AF.Exp,
                                                   scale=-0.5), S["act"])
                prev_trr = trr
                v.wait(trr)
                trb = v.run(lambda e: e.tensor_copy(rrowb[:], rrow[:]), S["dve"])
                t.wait(trb)
                if prev_trs is not None:
                    t.wait(prev_trs)
                tbc1 = t.run(lambda e: e.matmul(prr[:], onesb[:], rrowb[0:1, 0:512],
                                                start=True, stop=True), S["pe"])
                s.wait(tbc1)
                trs1 = s.run(lambda e: e.copy(rstdb[:, 0:512], prr[:]), S["act"])
                t.wait(trs1)
                tbc2 = t.run(lambda e: e.matmul(prr[:], onesb[:],
                                                rrowb[0:1, 512:1024],
                                                start=True, stop=True), S["pe"])
                s.wait(tbc2)
                trs = s.run(lambda e: e.copy(rstdb[:, 512:1024], prr[:]), S["act"])
                prev_trs = trs
                v.wait(trs)
                for grp in range(8):
                    v.run(lambda e, grp=grp, c=c: e.tensor_tensor(
                        regchunk(c)[:, 8 * grp:8 * (grp + 1), :],
                        regchunk(c)[:, 8 * grp:8 * (grp + 1), :],
                        normw[:, 8 * grp:8 * (grp + 1)].unsqueeze(2)
                        .broadcast_to([128, 8, CS]), op=ALU.mult))
                    v.run(lambda e, grp=grp, c=c: e.tensor_tensor(
                        regchunk(c)[:, 8 * grp:8 * (grp + 1), :],
                        regchunk(c)[:, 8 * grp:8 * (grp + 1), :],
                        rstdb[:, 128 * grp:128 * (grp + 1)].unsqueeze(1)
                        .broadcast_to([128, 8, CS]), op=ALU.mult))

                v.wait(tcdbc)
                v.wait(last_mm)
                tp1 = v.run(lambda e: e.tensor_tensor(
                    prevb[:].rearrange("p (h q) -> p h q", h=128),
                    prevb[:].rearrange("p (h q) -> p h q", h=128),
                    cdbc[:].unsqueeze(2).broadcast_to([128, 128, 64]),
                    op=ALU.mult), S["dve"])
                tok_cdbc_rd = tp1
                v.wait(tok_s0)
                tp2 = v.run(lambda e, c=c: e.tensor_add(
                    prevb[:, 0:4096], prevb[:, 0:4096], scb[c % 2][:]), S["dve"])
                g.wait(tp2)
                tok_s1 = g.run(lambda e, c=c: e.dma_start(
                    scb[c % 2][:], d_sc[c, :, 4096:8192]), S["scl"], 16)
                v.wait(tok_s1)
                tp3 = v.run(lambda e, c=c: e.tensor_add(
                    prevb[:, 4096:8192], prevb[:, 4096:8192], scb[c % 2][:]),
                    S["dve"])
                tok_s_used[c] = tp3

            for k in ["gld", "cbf0", "cbf1", "scl"]:
                g.wait(S[k].cur())
            emit_block(nc, E.values())

        with ExitStack() as ph:
            bwt = sb("a_bwt", [128, HID], BF16, ph)
            wos = [sb(f"a_wos{i}", [128, 512], BF16, ph) for i in range(2)]
            xres = [sb(f"a_xr{i}", [128, 512], F32, ph) for i in range(2)]
            ostg = [sb(f"a_os{i}", [128, 512], F32, ph) for i in range(2)]
            pout = [psum(f"p4_o{i}", [128, 512], F32, ph) for i in range(8)]

            E = engines()
            g, sy, v, s, t = E["gpsimd"], E["sync"], E["vector"], E["scalar"], E["tensor"]
            g.wait(S["act"].cur())
            g.wait(S["dve"].cur())
            g.wait(S["pe"].cur())
            t.wait(S["act"].cur())
            t.wait(S["dve"].cur())

            tbw = g.run(lambda e: e.dma_start(bwt[:], d_bwt), S["bwl"], 16)
            t.wait(tbw)
            tok_ev4 = {}
            tok_st4 = {}
            tok_mmk = {}
            for d in range(8):
                for tt in range(8):
                    if d >= 1:
                        t.wait(tok_ev4[(d - 1, tt)])
                    t.run(lambda e, d=d, tt=tt: e.matmul(
                        pout[tt][:], whfm[:, 128 * tt:128 * (tt + 1)],
                        bwt[:, 512 * d:512 * (d + 1)], start=True, stop=False))
                for k in range(64):
                    ki = 64 * d + k
                    if ki >= 2:
                        g.wait(tok_mmk.get(ki - 2))
                    twk = g.run(lambda e, d=d, k=k: e.dma_start(
                        wos[k % 2][:], d_wob[k][:, 512 * d:512 * (d + 1)]),
                        S["wos%d" % (k % 2)], 16)
                    t.wait(twk)
                    tk = None
                    for tt in range(8):
                        tk = t.run(lambda e, d=d, k=k, tt=tt: e.matmul(
                            pout[tt][:],
                            region[:, k * TOK + 128 * tt:k * TOK + 128 * (tt + 1)],
                            wos[k % 2][:], start=False, stop=(k == 63)),
                            S["pe"] if tt == 7 else None)
                    tok_mmk[ki] = tk
                for tt in range(8):
                    di = 8 * d + tt
                    if di >= 2:
                        prevtok = tok_ev4[(d, tt - 2)] if tt >= 2 else tok_ev4[(d - 1, 6 + tt)]
                        g.wait(prevtok)
                    txl = g.run(lambda e, d=d, tt=tt: e.dma_start(
                        xres[tt % 2][:], d_x[3 + 128 * tt:3 + 128 * (tt + 1),
                                            512 * d:512 * (d + 1)]),
                        S["xr%d" % (tt % 2)], 16)
                    v.wait(tok_mmk[64 * d + 63])
                    v.wait(txl)
                    if di >= 2:
                        prevst = tok_st4[(d, tt - 2)] if tt >= 2 else tok_st4[(d - 1, 6 + tt)]
                        v.wait(prevst)
                    tev = v.run(lambda e, d=d, tt=tt: e.tensor_add(
                        ostg[tt % 2][:], pout[tt][:], xres[tt % 2][:]), S["dve"])
                    tok_ev4[(d, tt)] = tev
                    sy.wait(tev)
                    tok_st4[(d, tt)] = sy.run(lambda e, d=d, tt=tt: e.dma_start(
                        d_out[128 * tt:128 * (tt + 1), 512 * d:512 * (d + 1)],
                        ostg[tt % 2][:]), S["out%d" % (tt % 2)], 16)

            for k in ["bwl", "wos0", "wos1", "xr0", "xr1"]:
                g.wait(S[k].cur())
            sy.wait(S["out0"].cur())
            sy.wait(S["out1"].cur())
            emit_block(nc, E.values())
        p23.close()
    return nc


def kernel(L0_attn_input_states, L0_attn_in_proj_w, L0_attn_conv_w,
           L0_attn_conv_b, L0_attn_dt_bias, L0_attn_A_log, L0_attn_D,
           L0_attn_norm_w, L0_attn_out_w, L0_ffn_route_w, L0_ffn_A, L0_ffn_Bw):
    x = np.asarray(L0_attn_input_states, np.float32)
    W = np.asarray(L0_attn_in_proj_w, np.float32)
    Wo = np.asarray(L0_attn_out_w, np.float32)
    cw = np.asarray(L0_attn_conv_w, np.float32)
    cb = np.asarray(L0_attn_conv_b, np.float32)
    dtb = np.asarray(L0_attn_dt_bias, np.float32)
    alog = np.asarray(L0_attn_A_log, np.float32)
    Dv = np.asarray(L0_attn_D, np.float32)
    nw = np.asarray(L0_attn_norm_w, np.float32)
    rw = np.asarray(L0_ffn_route_w, np.float32)
    Am = np.asarray(L0_ffn_A, np.float32)
    Bw = np.asarray(L0_ffn_Bw, np.float32)

    wt = np.ascontiguousarray(W.reshape(NCH, 128, NKT, 128).transpose(0, 3, 2, 1))
    wo = np.ascontiguousarray(Wo.T.reshape(64, 128, HID))
    convw = np.ascontiguousarray(
        cw[:, 0, :].reshape(NXBC, 128, KCONV).transpose(1, 0, 2).reshape(128, NXBC * KCONV))
    convb = np.ascontiguousarray(cb.reshape(NXBC, 128).T)
    dexp = np.ascontiguousarray(np.repeat(Dv.reshape(NCC, 2), 64, axis=1).T)
    normw = np.ascontiguousarray(nw.reshape(NCC, 128).T)
    routew = np.ascontiguousarray(rw.T.reshape(NKT, 128, 8))
    a_t = np.ascontiguousarray(Am.reshape(128, HID).T.reshape(NKT, 128, 128))
    bw_t = np.ascontiguousarray(Bw.transpose(0, 2, 1).reshape(128, HID))
    identb = np.eye(128, dtype=ml_dtypes.bfloat16)
    identf = np.eye(128, dtype=np.float32)
    jj, ii = np.meshgrid(np.arange(128), np.arange(128), indexing="ij")
    trilb = (jj <= ii).astype(ml_dtypes.bfloat16)
    mask01 = (ii >= jj).astype(np.float32)

    xf = x.reshape(2 * 4096, HID)
    in_maps = []
    for core in range(8):
        b, q = core // 4, core % 4
        lo = b * 4096 + q * 1024
        halo = np.zeros((3, HID), np.float32) if q == 0 else xf[lo - 3:lo]
        x_ext = np.ascontiguousarray(np.concatenate([halo, xf[lo:lo + 1024]], 0))
        am = np.zeros((128, 4), np.float32)
        am[:, :q] = 1.0
        in_maps.append(dict(
            x_ext=x_ext, wt=wt, wo=wo, convw=convw, convb=convb,
            dtb=np.ascontiguousarray(dtb.reshape(128, 1)),
            alog=np.ascontiguousarray(alog.reshape(128, 1)),
            dexp=dexp, normw=normw, routew=routew, a_t=a_t, bw_t=bw_t,
            identb=identb, identf=identf, trilb=trilb, mask01=mask01,
            amask=np.ascontiguousarray(am)))

    nc = build_program()
    res = run_bass_kernel_spmd(nc, in_maps, core_ids=list(range(8)))
    out = np.empty((2, 4096, HID), np.float32)
    for core in range(8):
        b, q = core // 4, core % 4
        out[b, q * 1024:(q + 1) * 1024, :] = res.results[core]["out"]
    return out


# version 05 (final)
# speedup vs baseline: 1.0165x; 1.0165x over previous
import numpy as np
import ml_dtypes
from contextlib import ExitStack

import concourse.bass as bass
import concourse.mybir as mybir
from concourse.bass_utils import run_bass_kernel_spmd

F32 = mybir.dt.float32
BF16 = mybir.dt.bfloat16
AF = mybir.ActivationFunctionType
ALU = mybir.AluOpType
AXX = mybir.AxisListType.X

HID = 4096
KCONV = 4
INTER = 8192
CONVC = 10240
PROJ = 18560
EPS = 1e-5

TOK = 1024
TOKE = TOK + 3
CS = 128
NCHK = 8
NKT = 32
NCH = 145
NCC = 64
NXBC = 80
TT1 = [343, 343, 341]


class Sem:
    def __init__(self, nc, st, name):
        self.h = st.enter_context(nc.semaphore(name))
        self.n = 0

    def next(self, k=1):
        self.n += k
        return (self.h, self.n)

    def cur(self):
        return (self.h, self.n)


class Eng:

    def __init__(self, name, auto=None):
        self.name = name
        self.ops = []
        self.auto = auto
        self.last = None

    def wait(self, tok):
        if tok is not None:
            self.ops.append(lambda e, t=tok: e.wait_ge(t[0], t[1]))

    def run(self, fn, inc=None, k=1):
        if self.auto is not None:
            if self.last is not None:
                self.wait(self.last)
            if inc is None:
                inc, k = self.auto, 1
        tok = inc.next(k) if inc is not None else None
        if inc is not None:
            self.ops.append(lambda e, f=fn, sh=inc.h, kk=k: f(e).then_inc(sh, kk))
        else:
            self.ops.append(lambda e, f=fn: f(e))
        if self.auto is not None:
            self.last = tok
        return tok


def emit_block(nc, engs):
    with nc.Block() as blk:
        for eng in engs:
            if not eng.ops:
                continue
            dec = getattr(blk, eng.name)
            ops = list(eng.ops)
            eng.ops.clear()

            def body(e, _ops=ops):
                for f in _ops:
                    f(e)
            dec(body)


def build_program():
    nc = bass.Bass("TRN2", target_bir_lowering=False)

    d_x = nc.dram_tensor("x_ext", [TOKE, HID], F32, kind="ExternalInput").ap()
    d_wt = nc.dram_tensor("wt", [NCH, 128, NKT, 128], F32, kind="ExternalInput").ap()
    d_wo = nc.dram_tensor("wo", [64, 128, HID], F32, kind="ExternalInput").ap()
    d_convw = nc.dram_tensor("convw", [128, NXBC * KCONV], F32, kind="ExternalInput").ap()
    d_convb = nc.dram_tensor("convb", [128, NXBC], F32, kind="ExternalInput").ap()
    d_dtb = nc.dram_tensor("dtb", [128, 1], F32, kind="ExternalInput").ap()
    d_alog = nc.dram_tensor("alog", [128, 1], F32, kind="ExternalInput").ap()
    d_dexp = nc.dram_tensor("dexp", [128, NCC], F32, kind="ExternalInput").ap()
    d_normw = nc.dram_tensor("normw", [128, NCC], F32, kind="ExternalInput").ap()
    d_rw = nc.dram_tensor("routew", [NKT, 128, 8], F32, kind="ExternalInput").ap()
    d_at = nc.dram_tensor("a_t", [NKT, 128, 128], F32, kind="ExternalInput").ap()
    d_bwt = nc.dram_tensor("bw_t", [128, HID], F32, kind="ExternalInput").ap()
    d_identb = nc.dram_tensor("identb", [128, 128], BF16, kind="ExternalInput").ap()
    d_identf = nc.dram_tensor("identf", [128, 128], F32, kind="ExternalInput").ap()
    d_trilb = nc.dram_tensor("trilb", [128, 128], BF16, kind="ExternalInput").ap()
    d_mask01 = nc.dram_tensor("mask01", [128, 128], F32, kind="ExternalInput").ap()
    d_amask = nc.dram_tensor("amask", [128, 4], F32, kind="ExternalInput").ap()
    d_out = nc.dram_tensor("out", [TOK, HID], F32, kind="ExternalOutput").ap()

    d_xbc = nc.dram_tensor("s_xbc", [NXBC, 128, TOKE], BF16, kind="Internal").ap()
    d_gate = nc.dram_tensor("s_gate", [NCHK, 128, NCC, CS], BF16, kind="Internal").ap()
    d_bc = nc.dram_tensor("s_bc", [NCHK, 128, 16, CS], BF16, kind="Internal").ap()
    d_wob = nc.dram_tensor("s_wob", [64, 128, HID], BF16, kind="Internal").ap()
    d_sc = nc.dram_tensor("s_sc", [NCHK, 128, INTER], BF16, kind="Internal").ap()
    d_ccin = nc.dram_tensor("s_ccin", [129, INTER], BF16, kind="Internal").ap()
    d_ccout = nc.dram_tensor("s_ccout", [4 * 129, INTER], BF16, kind="Internal").ap()

    top = ExitStack()
    with top:
        def sb(name, shape, dt, st=top):
            return st.enter_context(nc.sbuf_tensor(name, shape, dt))

        def psum(name, shape, dt, st=top):
            return st.enter_context(nc.psum_tensor(name, shape, dt))

        identb = sb("c_identb", [128, 128], BF16)
        identf = sb("c_identf", [128, 128], F32)
        trilb = sb("c_trilb", [128, 128], BF16)
        mask01 = sb("c_mask01", [128, 128], F32)
        convw = sb("c_convw", [128, NXBC * KCONV], F32)
        convb = sb("c_convb", [128, NXBC], F32)
        dtbias = sb("c_dtb", [128, 1], F32)
        negA = sb("c_negA", [128, 1], F32)
        dexp = sb("c_dexp", [128, NCC], F32)
        normw = sb("c_normw", [128, NCC], F32)
        amask = sb("c_amask", [128, 4], F32)
        onesb = sb("c_onesb", [1, 128], BF16)
        onescolb = sb("c_onescolb", [128, 1], BF16)
        lneps = sb("c_lneps", [128, 1], F32)
        rmseps = sb("c_rmseps", [1, 1], F32)
        onecol = sb("c_onecol", [128, 1], F32)

        whfm = sb("w_whfm", [128, TOK], BF16)

        dtf = sb("v_dtf", [128, TOKE], F32)
        dabf = sb("v_dabf", [128, TOK], BF16)
        lndt = sb("v_lndt", [128, TOK], F32)
        cumfm = sb("v_cumfm", [128, TOK], F32)
        cdec = sb("v_cdec", [128, NCHK], F32)
        dfac = sb("v_dfac", [128, NCHK], F32)
        dloccol = sb("v_dloc", [128, 1], F32)
        t8 = sb("v_t8", [128, NCHK], F32)
        gamcol = sb("v_gamcol", [128, 1], F32)
        rowb = sb("v_rowb", [1, 128], BF16)
        bcb = sb("v_bcb", [128, 128], BF16)
        prevb = sb("v_prev", [128, INTER], BF16)
        zrow = sb("v_zrow", [1, 224], F32)

        sems = ExitStack()
        top.enter_context(sems)
        S = {k: Sem(nc, sems, "sm_" + k) for k in
             ["pre", "p0x0", "p0x1", "ffn", "wb0", "wb1", "woi0", "woi1",
              "woo0", "woo1", "stp0", "stp1", "xin0", "xin1", "bcst",
              "bcg0", "bcg1", "scs", "cci", "cc", "gam", "sjp0", "sjp1",
              "gld", "cbf0", "cbf1", "scl", "bwl", "wos0", "wos1",
              "xr0", "xr1", "out0", "out1", "pe", "act", "dve"]}

        ENGS = {"sync": Eng("sync"), "gpsimd": Eng("gpsimd"),
                "scalar": Eng("scalar", auto=S["act"]),
                "vector": Eng("vector", auto=S["dve"]),
                "tensor": Eng("tensor")}

        def engines():
            return ENGS

        cb_state = {"prow_rd": None, "bcb_rd": None, "prow": None, "pbc": None}

        def colbcast(t, s, col_ap, in_tok=None):
            prow_ap = cb_state["prow"]
            pbc_ap = cb_state["pbc"]
            t.wait(in_tok)
            if cb_state["prow_rd"] is not None:
                t.wait(cb_state["prow_rd"])
            tk1 = t.run(lambda e, a=col_ap, o=prow_ap: e.transpose(o, a, identf[:]),
                        S["pe"])
            s.wait(tk1)
            if cb_state["bcb_rd"] is not None:
                s.wait(cb_state["bcb_rd"])
            tk2 = s.run(lambda e, i=prow_ap: e.copy(rowb[:], i), S["act"])
            cb_state["prow_rd"] = tk2
            t.wait(tk2)
            tk3 = t.run(lambda e, o=pbc_ap: e.matmul(o, onesb[:], rowb[:],
                                                     start=True, stop=True), S["pe"])
            s.wait(tk3)
            tk4 = s.run(lambda e, i=pbc_ap: e.copy(bcb[:], i), S["act"])
            t.wait(tk4)
            return tk4

        E = engines()
        g, sy, v, s, t = E["gpsimd"], E["sync"], E["vector"], E["scalar"], E["tensor"]
        tok = None
        for tsb, src in [(identb, d_identb), (identf, d_identf), (trilb, d_trilb),
                         (mask01, d_mask01), (convw, d_convw), (convb, d_convb),
                         (dtbias, d_dtb), (dexp, d_dexp), (normw, d_normw),
                         (amask, d_amask), (negA, d_alog)]:
            tok = g.run(lambda e, a=tsb, b=src: e.dma_start(a[:], b), S["pre"], 16)
        g.wait(tok)
        v.wait(tok)
        v.run(lambda e: e.memset(onesb[:], 1.0))
        v.run(lambda e: e.memset(onescolb[:], 1.0))
        v.run(lambda e: e.memset(lneps[:], EPS))
        v.run(lambda e: e.memset(rmseps[:], EPS))
        v.run(lambda e: e.memset(onecol[:], 1.0))
        v.run(lambda e: e.memset(prevb[:], 0.0))
        v.run(lambda e: e.memset(zrow[:], 0.0))
        s.wait(tok)
        s.run(lambda e: e.activation(negA[:], negA[:], AF.Exp))
        s.run(lambda e: e.mul(negA[:], negA[:], -1.0))
        emit_block(nc, E.values())

        p01 = ExitStack()
        xnf = sb("a_xnf", [128, NKT * TOKE], BF16, p01)
        with ExitStack() as ph:
            sx = [sb(f"a_sx{i}", [128, HID], F32, ph) for i in range(2)]
            xnts = [sb(f"a_xnts{i}", [128, HID], BF16, ph) for i in range(2)]
            bnst = sb("a_bnst", [128, 48], F32, ph)
            mcol = sb("a_mcol", [128, 1], F32, ph)
            vcol = sb("a_vcol", [128, 1], F32, ph)
            rcol = sb("a_rcol", [128, 1], F32, ph)
            m16 = sb("a_m16", [128, 16], F32, ph)
            ptr = [psum(f"p0_tr{i}", [128, 512], BF16, ph) for i in range(2)]

            E = engines()
            g, sy, v, s, t = E["gpsimd"], E["sync"], E["vector"], E["scalar"], E["tensor"]
            rows = [128] * 8 + [3]
            tok_xn = [None] * 9
            tok_trl = [None] * 9
            tok_evac = []

            def stat(r, idx):
                return bnst[0:r, :].rearrange("p (a b) -> p a b", b=3)[:, :, idx]

            for i in range(9):
                r = rows[i]
                if i >= 2:
                    g.wait(tok_xn[i - 2])
                tld = g.run(lambda e, i=i, r=r: e.dma_start(
                    sx[i % 2][0:r, :], d_x[128 * i:128 * i + r, :]),
                    S["p0x%d" % (i % 2)], 16)
                v.wait(tld)
                for s8 in range(8):
                    v.run(lambda e, i=i, r=r, s8=s8: e.bn_stats(
                        bnst[0:r, 6 * s8:6 * s8 + 6],
                        sx[i % 2][0:r, 512 * s8:512 * (s8 + 1)]))
                v.run(lambda e, r=r: e.reduce_sum(mcol[0:r, :], stat(r, 1), axis=AXX))
                v.run(lambda e, r=r: e.tensor_scalar_mul(mcol[0:r, :], mcol[0:r, :], 1.0 / 16.0))
                v.run(lambda e, r=r: e.reduce_sum(vcol[0:r, :], stat(r, 2), axis=AXX))
                v.run(lambda e, r=r: e.tensor_scalar_sub(m16[0:r, :], stat(r, 1), mcol[0:r, :]))
                v.run(lambda e, r=r: e.tensor_mul(m16[0:r, :], m16[0:r, :], m16[0:r, :]))
                v.run(lambda e, r=r: e.reduce_sum(rcol[0:r, :], m16[0:r, :], axis=AXX))
                tv = v.run(lambda e, r=r: e.scalar_tensor_tensor(
                    vcol[0:r, :], rcol[0:r, :], 256.0, vcol[0:r, :],
                    op0=ALU.mult, op1=ALU.add), S["dve"])
                s.wait(tv)
                s.run(lambda e, r=r: e.activation(rcol[0:r, :], vcol[0:r, :], AF.Ln,
                                                  bias=lneps[0:r, :], scale=1.0 / 4096.0))
                ta = s.run(lambda e, r=r: e.activation(rcol[0:r, :], rcol[0:r, :],
                                                       AF.Exp, scale=-0.5), S["act"])
                v.wait(ta)
                if i >= 2:
                    v.wait(tok_trl[i - 2])
                tok_xn[i] = v.run(lambda e, i=i, r=r: e.tensor_scalar(
                    xnts[i % 2][0:r, :], sx[i % 2][0:r, :], mcol[0:r, :], rcol[0:r, :],
                    op0=ALU.subtract, op1=ALU.mult), S["dve"])
                t.wait(tok_xn[i])
                for kb in range(8):
                    gi = 8 * i + kb
                    if gi >= 2:
                        t.wait(tok_evac[gi - 2])
                    tk = None
                    for k4 in range(4):
                        k = 4 * kb + k4
                        tk = t.run(lambda e, i=i, r=r, kb=kb, k4=k4, k=k: e.transpose(
                            ptr[kb % 2][0:128, 128 * k4:128 * k4 + r],
                            xnts[i % 2][0:r, 128 * k:128 * (k + 1)], identb[0:r, 0:r]),
                            S["pe"] if k4 == 3 else None)
                    tok_trl[i] = tk
                    s.wait(tk)
                    te = None
                    for k4 in range(4):
                        k = 4 * kb + k4
                        te = s.run(lambda e, i=i, r=r, kb=kb, k4=k4, k=k: e.copy(
                            xnf[0:128, k * TOKE + 128 * i: k * TOKE + 128 * i + r],
                            ptr[kb % 2][0:128, 128 * k4:128 * k4 + r]),
                            S["act"] if k4 == 3 else None)
                    tok_evac.append(te)
            emit_block(nc, E.values())

        with ExitStack() as ph:
            wb = [sb(f"a_wb{i}", [128, NKT * 128], BF16, ph) for i in range(2)]
            pstage = [sb(f"a_pq{i}", [128, TOKE], BF16, ph) for i in range(2)]
            wob = [sb(f"a_wob{i}", [128, HID], BF16, ph) for i in range(2)]
            ffnw = sb("a_ffnw", [128, NKT * 136], BF16, ph)
            rwl = sb("a_rwl", [128, 8], F32, ph)
            maxc = sb("a_maxc", [128, 1], F32, ph)
            rweq = sb("a_rweq", [128, 8], F32, ph)
            whtk = sb("a_whtk", [128, 128], BF16, ph)
            pmm = [psum(f"p1_mm{i}{j}", [128, TT1[j]], F32, ph)
                   for i in range(2) for j in range(3)]
            pf = psum("p1_f", [128, 512], F32, ph)
            pfl = pf[:, 0:8]
            pfh = pf[:, 128:256]
            pwh = psum("p1_wh", [128, 128], BF16, ph)

            E = engines()
            g, sy, v, s, t = E["gpsimd"], E["sync"], E["vector"], E["scalar"], E["tensor"]
            t.wait(S["act"].cur())
            g.wait(S["act"].cur())
            g.wait(S["dve"].cur())
            g.wait(S["pe"].cur())

            g.run(lambda e: e.dma_start(
                ffnw[:].rearrange("p (k f) -> p k f", k=NKT)[:, :, 0:128],
                d_at.rearrange("k d f -> d k f")), S["ffn"], 16)
            tff = g.run(lambda e: e.dma_start(
                ffnw[:].rearrange("p (k f) -> p k f", k=NKT)[:, :, 128:136],
                d_rw.rearrange("k d f -> d k f")), S["ffn"], 16)

            wo_sched = {}
            for i in range(64):
                wo_sched.setdefault(min(NCH - 1, int(i * NCH / 64.0)), []).append(i)
            tok_woout = {}
            tok_grp = {}
            tok_ev = {}
            tok_sp = {}
            for c in range(NCH):
                if c >= 2:
                    g.wait(tok_grp[(c - 2, 2)])
                twl = g.run(lambda e, c=c: e.dma_start(
                    wb[c % 2][:].rearrange("p (k f) -> p k f", k=NKT),
                    d_wt[c]), S["wb%d" % (c % 2)], 16)
                for i in wo_sched.get(c, []):
                    if i >= 2:
                        g.wait(tok_woout[i - 2])
                    twi = g.run(lambda e, i=i: e.dma_start(
                        wob[i % 2][:], d_wo[i]), S["woi%d" % (i % 2)], 16)
                    sy.wait(twi)
                    tok_woout[i] = sy.run(lambda e, i=i: e.dma_start(
                        d_wob[i], wob[i % 2][:]), S["woo%d" % (i % 2)], 16)
                t.wait(twl)
                for j in range(3):
                    t0 = sum(TT1[:j])
                    if c >= 2:
                        t.wait(tok_ev[(c - 2, j)])
                    tk = None
                    for k in range(NKT):
                        tk = t.run(lambda e, c=c, j=j, k=k, t0=t0: e.matmul(
                            pmm[(c % 2) * 3 + j][:],
                            wb[c % 2][:, 128 * k:128 * (k + 1)],
                            xnf[:, k * TOKE + t0: k * TOKE + t0 + TT1[j]],
                            start=(k == 0), stop=(k == NKT - 1)),
                            S["pe"] if k == NKT - 1 else None)
                    tok_grp[(c, j)] = tk
                for j in range(3):
                    t0 = sum(TT1[:j])
                    s.wait(tok_grp[(c, j)])
                    if c >= 2 and (c - 2, 2) in tok_sp:
                        s.wait(tok_sp[(c - 2, 2)])
                    if c < NCC:
                        te = s.run(lambda e, c=c, j=j, t0=t0: e.activation(
                            pstage[c % 2][:, t0:t0 + TT1[j]],
                            pmm[(c % 2) * 3 + j][:], AF.Silu), S["act"])
                    elif c < NCC + NXBC:
                        te = s.run(lambda e, c=c, j=j, t0=t0: e.copy(
                            pstage[c % 2][:, t0:t0 + TT1[j]],
                            pmm[(c % 2) * 3 + j][:]), S["act"])
                    else:
                        te = s.run(lambda e, c=c, j=j, t0=t0: e.copy(
                            dtf[:, t0:t0 + TT1[j]],
                            pmm[(c % 2) * 3 + j][:]), S["act"])
                    tok_ev[(c, j)] = te
                if c < NCC + NXBC:
                    for j in range(3):
                        t0 = sum(TT1[:j])
                        sy.wait(tok_ev[(c, j)])
                        if c < NCC:
                            lo = max(t0, 3)
                            hi = t0 + TT1[j]
                            tsp = None
                            while lo < hi:
                                ch = (lo - 3) // CS
                                nxt = min(hi, 3 + (ch + 1) * CS)
                                tsp = sy.run(lambda e, c=c, lo=lo, nxt=nxt, ch=ch:
                                             e.dma_start(
                                    d_gate[ch, :, c,
                                           (lo - 3) - ch * CS:(nxt - 3) - ch * CS],
                                    pstage[c % 2][:, lo:nxt]),
                                    S["stp%d" % (c % 2)], 16)
                                lo = nxt
                        else:
                            tsp = sy.run(lambda e, c=c, j=j, t0=t0: e.dma_start(
                                d_xbc[c - NCC, :, t0:t0 + TT1[j]],
                                pstage[c % 2][:, t0:t0 + TT1[j]]),
                                S["stp%d" % (c % 2)], 16)
                        tok_sp[(c, j)] = tsp

            t.wait(tff)
            tw_prev = None
            tev_prev = None
            for tt in range(8):
                if tw_prev is not None:
                    t.wait(tw_prev)
                tkh = None
                for k in range(NKT):
                    t.run(lambda e, tt=tt, k=k: e.matmul(
                        pfl,
                        xnf[:, k * TOKE + 3 + 128 * tt: k * TOKE + 3 + 128 * (tt + 1)],
                        ffnw[:, k * 136 + 128: k * 136 + 136],
                        start=(k == 0), stop=(k == NKT - 1), skip_group_check=True))
                    tkh = t.run(lambda e, tt=tt, k=k: e.matmul(
                        pfh,
                        xnf[:, k * TOKE + 3 + 128 * tt: k * TOKE + 3 + 128 * (tt + 1)],
                        ffnw[:, k * 136: k * 136 + 128],
                        start=(k == 0), stop=(k == NKT - 1), skip_group_check=True),
                        S["pe"] if k == NKT - 1 else None)
                v.wait(tkh)
                v.run(lambda e: e.tensor_copy(rwl[:], pfl))
                v.run(lambda e: e.reduce_max(maxc[:], rwl[:], axis=AXX))
                v.run(lambda e: e.tensor_single_scalar(rweq[:], rwl[:], maxc[:],
                                                       op=ALU.is_equal))
                v.run(lambda e: e.tensor_single_scalar(rweq[:], rweq[:], maxc[:],
                                                       op=ALU.mult))
                tw_prev = v.run(lambda e: e.tensor_tensor(
                    whtk[:].rearrange("p (a b) -> p a b", a=8),
                    pfh.rearrange("p (a b) -> p a b", a=8),
                    rweq[:].unsqueeze(2).broadcast_to([128, 8, 16]),
                    op=ALU.mult), S["dve"])
                t.wait(tw_prev)
                if tev_prev is not None:
                    t.wait(tev_prev)
                ttr = t.run(lambda e: e.transpose(pwh[:], whtk[:], identb[:]), S["pe"])
                s.wait(ttr)
                tev_prev = s.run(lambda e, tt=tt: e.copy(
                    whfm[:, 128 * tt:128 * (tt + 1)], pwh[:]), S["act"])

            for k in ["wb0", "wb1", "woi0", "woi1", "ffn"]:
                g.wait(S[k].cur())
            for k in ["woo0", "woo1", "stp0", "stp1"]:
                sy.wait(S[k].cur())
            emit_block(nc, E.values())
        p01.close()

        p23 = ExitStack()
        region = sb("a_region", [128, NCC * TOK], BF16, p23)

        def regv(cc, lo, n):
            return region[:, cc * TOK + lo: cc * TOK + lo + n]

        def regchunk(c):
            return region[:].rearrange("p (a b) -> p a b", a=NCC)[:, :, c * CS:(c + 1) * CS]

        tok_cv_out = [None] * NXBC
        with ExitStack() as ph:
            xin = [sb(f"a_xin{i}", [128, TOKE], BF16, ph) for i in range(2)]
            cvts = [sb(f"a_cvt{i}", [128, TOK], BF16, ph) for i in range(2)]
            bstg = sb("a_bstg", [128, TOK], BF16, ph)

            E = engines()
            g, sy, v, s, t = E["gpsimd"], E["sync"], E["vector"], E["scalar"], E["tensor"]
            for k in ["stp0", "stp1", "woo0", "woo1"]:
                g.wait(S[k].cur())
            g.wait(S["act"].cur())
            g.wait(S["dve"].cur())
            g.wait(S["pe"].cur())
            tok_cv_dve = [None] * NXBC
            tok_bc_sp = None
            for cc in range(NXBC):
                if cc >= 2:
                    g.wait(tok_cv_dve[cc - 2])
                tld = g.run(lambda e, cc=cc: e.dma_start(
                    xin[cc % 2][:], d_xbc[cc]), S["xin%d" % (cc % 2)], 16)
                v.wait(tld)
                if cc >= 2:
                    v.wait(tok_cv_out[cc - 2])
                v.run(lambda e, cc=cc: e.tensor_single_scalar(
                    cvts[cc % 2][:], xin[cc % 2][:, 0:TOK], convw[:, 4 * cc:4 * cc + 1],
                    op=ALU.mult))
                tk = None
                for k in range(1, 4):
                    tk = v.run(lambda e, cc=cc, k=k: e.scalar_tensor_tensor(
                        cvts[cc % 2][:], xin[cc % 2][:, k:k + TOK],
                        convw[:, 4 * cc + k:4 * cc + k + 1],
                        cvts[cc % 2][:], op0=ALU.mult, op1=ALU.add),
                        S["dve"] if k == 3 else None)
                tok_cv_dve[cc] = tk
                s.wait(tk)
                if cc < NCC:
                    tok_cv_out[cc] = s.run(lambda e, cc=cc: e.activation(
                        regv(cc, 0, TOK), cvts[cc % 2][:], AF.Silu,
                        bias=convb[:, cc:cc + 1]), S["act"])
                else:
                    if tok_bc_sp is not None:
                        s.wait(tok_bc_sp)
                    tso = s.run(lambda e, cc=cc: e.activation(
                        bstg[:], cvts[cc % 2][:], AF.Silu,
                        bias=convb[:, cc:cc + 1]), S["act"])
                    sy.wait(tso)
                    for ch in range(NCHK):
                        tok_bc_sp = sy.run(lambda e, cc=cc, ch=ch: e.dma_start(
                            d_bc[ch, :, cc - NCC, :],
                            bstg[:, ch * CS:(ch + 1) * CS]), S["bcst"], 16)
                    tok_cv_out[cc] = tok_bc_sp
            g.wait(S["xin0"].cur())
            g.wait(S["xin1"].cur())
            sy.wait(S["bcst"].cur())
            emit_block(nc, E.values())

        with ExitStack() as ph:
            sloc = sb("a_sloc", [128, INTER], BF16, ph)
            xtok = sb("a_xtok", [128, 1024], BF16, ph)
            xsg = sb("a_xsg", [128, 1024], BF16, ph)
            btok = sb("a_btok", [128, 128], BF16, ph)
            bcg = [sb(f"a_bcg{i}", [128, 2048], BF16, ph) for i in range(2)]
            dfbc = sb("a_dfbc", [128, 128], BF16, ph)
            stmp = sb("a_stmp", [128, 1024], BF16, ph)
            wtokb = sb("a_wtokb", [128, 128], BF16, ph)
            dlastc = sb("a_dlastc", [128, 128], F32, ph)
            wfmc = sb("a_wfmc", [128, 128], F32, ph)
            datok = sb("a_datok", [128, 128], BF16, ph)
            cumtk = sb("a_cumtk", [128, 128], F32, ph)
            biasmf = sb("a_biasmf", [128, 128], F32, ph)
            biastk = sb("a_biastk", [128, 128], F32, ph)
            cumrep = sb("a_cumrep", [128, 512], F32, ph)
            e4one = sb("a_e4", [128, 512], F32, ph)
            e4 = [e4one, e4one]
            sc4 = [sb(f"a_sc4{i}", [128, 512], BF16, ph) for i in range(2)]
            grammone = sb("a_gramm", [128, 128], F32, ph)
            gramm = [grammone, grammone]
            scst = sb("a_scst", [128, 1024], BF16, ph)
            sjp = bcg

            pmisc = psum("p2_misc", [128, 512], F32, ph)
            pcm0 = pmisc[:, 0:128]
            pgram0 = pmisc[:, 128:256]
            cb_state["pbc"] = pmisc[:, 256:384]
            cb_state["prow"] = pmisc[0:1, 384:512]
            ptrb = [psum(f"p2_trb{i}", [128, 512], BF16, ph) for i in range(2)]
            pcb = [psum(f"p2_cb{i}", [128, 512], F32, ph) for i in range(2)]
            pst0 = psum("p2_st", [128, 512], F32, ph)
            py = [psum(f"p2_y{i}", [128, 256], F32, ph) for i in range(2)]

            E = engines()
            g, sy, v, s, t = E["gpsimd"], E["sync"], E["vector"], E["scalar"], E["tensor"]
            t.wait(S["act"].cur())
            t.wait(S["dve"].cur())
            g.wait(S["act"].cur())
            g.wait(S["dve"].cur())
            g.wait(S["pe"].cur())

            s.run(lambda e: e.add(dtf[:], dtf[:], dtbias[:]))
            tsp0 = s.run(lambda e: e.activation(lndt[:], dtf[:, 3:TOKE], AF.Abs), S["act"])
            s.run(lambda e: e.activation(lndt[:], lndt[:], AF.Exp, scale=-1.0))
            tsp1 = s.run(lambda e: e.activation(lndt[:], lndt[:], AF.Ln,
                                                bias=onecol[:]), S["act"])
            v.wait(tsp0)
            v.run(lambda e: e.tensor_scalar_max(cumfm[:], dtf[:, 3:TOKE], 0.0))
            v.wait(tsp1)
            tdt = v.run(lambda e: e.tensor_add(dtf[:, 3:TOKE], cumfm[:], lndt[:]), S["dve"])
            v.run(lambda e: e.tensor_single_scalar(dabf[:], dtf[:, 3:TOKE], negA[:],
                                                   op=ALU.mult))
            tda = v.run(lambda e: e.tensor_copy(dabf[0:1, 0:1], dabf[0:1, 0:1]), S["dve"])
            s.wait(tdt)
            tlnd = s.run(lambda e: e.activation(lndt[:], dtf[:, 3:TOKE], AF.Ln), S["act"])

            tok_cum = [None] * NCHK
            for c in range(NCHK):
                t.wait(tda)
                ttr = t.run(lambda e, c=c: e.transpose(
                    ptrb[0][:, 0:128], dabf[:, c * CS:(c + 1) * CS], identb[:]), S["pe"])
                s.wait(ttr)
                tevd = s.run(lambda e: e.copy(datok[:], ptrb[0][:, 0:128]), S["act"])
                t.wait(tevd)
                if c > 0:
                    t.wait(tok_cum[c - 1])
                tmm = t.run(lambda e, c=c: e.matmul(
                    pcm0, datok[:], trilb[:], start=True, stop=True,
                    skip_group_check=True), S["pe"])
                s.wait(tmm)
                tok_cum[c] = s.run(lambda e, c=c: e.copy(
                    cumfm[:, c * CS:(c + 1) * CS], pcm0), S["act"])
            s.wait(tok_cum[NCHK - 1])
            s.run(lambda e: e.activation(
                cdec[:], cumfm[:].rearrange("p (c i) -> p c i", c=NCHK)[:, :, 127],
                AF.Exp))
            v.wait(tok_cum[NCHK - 1])
            v.run(lambda e: e.memset(t8[:, NCHK - 1:NCHK], 0.0))
            for c in range(NCHK - 2, -1, -1):
                v.run(lambda e, c=c: e.tensor_add(
                    t8[:, c:c + 1], t8[:, c + 1:c + 2],
                    cumfm[:, (c + 1) * CS + 127:(c + 1) * CS + 128]))
            tvs = v.run(lambda e: e.tensor_add(
                dloccol[:], t8[:, 0:1], cumfm[:, 127:128]), S["dve"])
            v.run(lambda e: e.memset(sloc[:], 0.0))
            s.wait(tvs)
            s.run(lambda e: e.activation(dfac[:], t8[:], AF.Exp))
            tdfac = s.run(lambda e: e.activation(dloccol[:], dloccol[:], AF.Exp), S["act"])

            tok_bcld = [None] * NCHK
            tok_bcg_rd = {}
            ptrb_rd = {0: tevd, 1: None}
            py_tok = {}
            sc_tok = {}
            e4_exp_tok = None
            tok_biastk_rd = None
            pcb_tok = {}
            tok_cumrep_rd = None
            tok_cumtk_rd = None
            tok_wtok_rd = None
            tok_dfbc_rd = None
            tok_scst_dve = None
            tok_scst_sp = None
            tok_btok_rd = None
            tok_gram_rd = None
            tok_xtok_rd = []
            for c in range(NCHK):
                g.wait(tok_cv_out[NXBC - 1])
                if c >= 2:
                    g.wait(tok_bcg_rd[c - 2])
                tok_bcld[c] = g.run(lambda e, c=c: e.dma_start(
                    bcg[c % 2][:].rearrange("p (a b) -> p a b", a=16),
                    d_bc[c]), S["bcg%d" % (c % 2)], 16)

                tdfbcv = colbcast(t, s, dfac[:, c:c + 1], in_tok=tdfac)
                v.wait(tdfbcv)
                if tok_dfbc_rd is not None:
                    v.wait(tok_dfbc_rd)
                tvcp = v.run(lambda e: e.tensor_copy(dfbc[:], bcb[:]), S["dve"])
                cb_state["bcb_rd"] = tvcp

                if tok_wtok_rd is not None:
                    s.wait(tok_wtok_rd)
                if c == 0:
                    t.wait(tok_cum[NCHK - 1])
                else:
                    t.wait(tok_pcm0_rd)
                tdl = s.run(lambda e, c=c: e.activation(
                    dlastc[:], cumfm[:, c * CS:(c + 1) * CS], AF.Exp, scale=-1.0,
                    bias=cumfm[:, c * CS + 127:c * CS + 128]), S["act"])
                v.wait(tdl)
                twf = v.run(lambda e, c=c: e.tensor_mul(
                    wfmc[:], dlastc[:], dtf[:, 3 + c * CS:3 + (c + 1) * CS]), S["dve"])
                t.wait(twf)
                tw1 = t.run(lambda e: e.transpose(pcm0, wfmc[:], identf[:]), S["pe"])
                s.wait(tw1)
                tw2 = s.run(lambda e: e.copy(wtokb[:], pcm0), S["act"])

                v.wait(tok_cum[c])
                v.wait(tlnd)
                tbm = v.run(lambda e, c=c: e.tensor_sub(
                    biasmf[:], lndt[:, c * CS:(c + 1) * CS],
                    cumfm[:, c * CS:(c + 1) * CS]), S["dve"])
                t.wait(tbm)
                t.wait(tw2)
                tb1 = t.run(lambda e: e.transpose(pcm0, biasmf[:], identf[:]), S["pe"])
                s.wait(tb1)
                if tok_biastk_rd is not None:
                    s.wait(tok_biastk_rd)
                tb2 = s.run(lambda e: e.copy(biastk[:], pcm0), S["act"])
                t.wait(tb2)
                tc1 = t.run(lambda e, c=c: e.transpose(
                    pcm0, cumfm[:, c * CS:(c + 1) * CS], identf[:]), S["pe"])
                s.wait(tc1)
                if tok_cumtk_rd is not None:
                    s.wait(tok_cumtk_rd)
                tc2 = s.run(lambda e, c=c: e.copy(cumtk[:], pcm0), S["act"])
                tok_pcm0_rd = tc2

                for gg in range(8):
                    txe = None
                    first_evac = True
                    for q in range(8):
                        cc = 8 * gg + q
                        t.wait(tok_cv_out[cc])
                        pq = q // 4
                        if ptrb_rd[pq] is not None:
                            t.wait(ptrb_rd[pq])
                            ptrb_rd[pq] = None
                        tk = t.run(lambda e, cc=cc, q=q, c=c: e.transpose(
                            ptrb[q // 4][:, 128 * (q % 4):128 * (q % 4 + 1)],
                            regv(cc, c * CS, CS), identb[:]),
                            S["pe"] if q % 4 == 3 else None)
                        if q % 4 == 3:
                            s.wait(tk)
                            if first_evac:
                                for tkx in tok_xtok_rd:
                                    s.wait(tkx)
                                tok_xtok_rd = []
                                first_evac = False
                            txe = s.run(lambda e, q=q: e.copy(
                                xtok[:, 512 * (q // 4):512 * (q // 4) + 512],
                                ptrb[q // 4][:]), S["act"])
                            ptrb_rd[q // 4] = txe
                    v.wait(txe)
                    v.wait(tw2)
                    if tok_scst_dve is not None:
                        v.wait(tok_scst_dve)
                    txs = v.run(lambda e, gg=gg: e.tensor_tensor(
                        xsg[:].rearrange("p (h q) -> p h q", h=16),
                        xtok[:].rearrange("p (h q) -> p h q", h=16),
                        wtokb[:, 16 * gg:16 * (gg + 1)].unsqueeze(2)
                        .broadcast_to([128, 16, 64]), op=ALU.mult), S["dve"])
                    tok_wtok_rd = txs
                    tok_xtok_rd.append(txs)
                    t.wait(tok_bcld[c])
                    if ptrb_rd[0] is not None:
                        t.wait(ptrb_rd[0])
                        ptrb_rd[0] = None
                    if tok_btok_rd is not None:
                        t.wait(tok_btok_rd)
                    tbt = t.run(lambda e, gg=gg, c=c: e.transpose(
                        ptrb[0][:, 0:128], bcg[c % 2][:, gg * 128:(gg + 1) * 128],
                        identb[:]), S["pe"])
                    s.wait(tbt)
                    tbte = s.run(lambda e: e.copy(btok[:], ptrb[0][:, 0:128]), S["act"])
                    ptrb_rd[0] = tbte
                    if tok_gram_rd is not None:
                        t.wait(tok_gram_rd)
                    tgr = t.run(lambda e, gg=gg, c=c: e.matmul(
                        pgram0, bcg[c % 2][:, gg * 128:(gg + 1) * 128],
                        bcg[c % 2][:, 1024 + gg * 128:1024 + (gg + 1) * 128],
                        start=True, stop=True, skip_group_check=True), S["pe"])
                    tok_bcg_rd[c] = tgr
                    v.wait(tgr)
                    tgm = v.run(lambda e, gg=gg: e.tensor_mul(
                        gramm[gg % 2][:], pgram0, mask01[:]), S["dve"])
                    tok_gram_rd = tgm
                    t.wait(txs)
                    t.wait(tbte)
                    tse = None
                    for half in range(2):
                        tst = None
                        for hh in range(8):
                            hloc = half * 8 + hh
                            tst = t.run(lambda e, half=half, hh=hh, hloc=hloc: e.matmul(
                                pst0[:, 64 * hh:64 * (hh + 1)], btok[:],
                                xsg[:, 64 * hloc:64 * (hloc + 1)],
                                start=True, stop=True),
                                S["pe"] if hh == 7 else None)
                        s.wait(tst)
                        if tok_scst_sp is not None:
                            s.wait(tok_scst_sp)
                        if tok_scst_dve is not None:
                            s.wait(tok_scst_dve)
                        tse = s.run(lambda e, half=half: e.copy(
                            scst[:, 512 * half:512 * (half + 1)], pst0[:]),
                            S["act"])
                        t.wait(tse)
                    tok_btok_rd = tst
                    sy.wait(tse)
                    tok_scst_sp = sy.run(lambda e, c=c, gg=gg: e.dma_start(
                        d_sc[c, :, 1024 * gg:1024 * (gg + 1)], scst[:]), S["scs"], 16)
                    v.wait(tse)
                    v.wait(tvcp)
                    v.run(lambda e, gg=gg: e.tensor_tensor(
                        stmp[:].rearrange("p (h q) -> p h q", h=16),
                        scst[:].rearrange("p (h q) -> p h q", h=16),
                        dfbc[:, 16 * gg:16 * (gg + 1)].unsqueeze(2)
                        .broadcast_to([128, 16, 64]), op=ALU.mult))
                    tok_scst_dve = v.run(lambda e, gg=gg: e.tensor_add(
                        sloc[:, 1024 * gg:1024 * (gg + 1)],
                        sloc[:, 1024 * gg:1024 * (gg + 1)], stmp[:]), S["dve"])
                    tok_dfbc_rd = tok_scst_dve

                    for hb in range(4):
                        h0 = 16 * gg + 4 * hb
                        bi = hb % 2
                        v.wait(tc2)
                        if tok_cumrep_rd is not None:
                            v.wait(tok_cumrep_rd)
                        tcr = v.run(lambda e, h0=h0: e.tensor_copy(
                            cumrep[:].rearrange("p (a b) -> p a b", a=4),
                            cumtk[:, h0 % 128:h0 % 128 + 4].unsqueeze(2)
                            .broadcast_to([128, 4, 128])), S["dve"])
                        tok_cumtk_rd = tcr
                        t.wait(tcr)
                        if pcb_tok.get(bi) is not None:
                            t.wait(pcb_tok[bi])
                        tk = None
                        for q in range(4):
                            tk = t.run(lambda e, q=q, bi=bi: e.transpose(
                                pcb[bi][:, 128 * q:128 * (q + 1)],
                                cumrep[:, 128 * q:128 * (q + 1)], identf[:]),
                                S["pe"] if q == 3 else None)
                        tok_cumrep_rd = tk
                        v.wait(tk)
                        v.wait(tb2)
                        if e4_exp_tok is not None:
                            v.wait(e4_exp_tok)
                        tva = None
                        for q in range(4):
                            hh = h0 + q
                            tva = v.run(lambda e, q=q, hh=hh, bi=bi: e.tensor_scalar(
                                e4[bi][:, 128 * q:128 * (q + 1)],
                                pcb[bi][:, 128 * q:128 * (q + 1)],
                                biastk[:, hh % 128:hh % 128 + 1], 30.0,
                                op0=ALU.add, op1=ALU.min),
                                S["dve"] if q == 3 else None)
                        pcb_tok[bi] = tva
                        tok_biastk_rd = tva
                        s.wait(tva)
                        ta = s.run(lambda e, bi=bi: e.activation(
                            e4[bi][:], e4[bi][:], AF.Exp), S["act"])
                        e4_exp_tok = ta
                        v.wait(ta)
                        v.wait(tgm)
                        if sc_tok.get(bi) is not None:
                            v.wait(sc_tok[bi])
                        tsc = v.run(lambda e, bi=bi, gg=gg: e.tensor_tensor(
                            sc4[bi][:].rearrange("p (a b) -> p a b", a=4),
                            e4[bi][:].rearrange("p (a b) -> p a b", a=4),
                            gramm[gg % 2][:].unsqueeze(1).broadcast_to([128, 4, 128]),
                            op=ALU.mult), S["dve"])
                        t.wait(tsc)
                        if py_tok.get(bi) is not None:
                            t.wait(py_tok[bi])
                        tky = None
                        for q in range(4):
                            hh = h0 + q
                            hloc = 4 * hb + q
                            tky = t.run(lambda e, q=q, hh=hh, hloc=hloc, bi=bi, gg=gg:
                                        e.matmul(
                                py[bi][64 * (hh % 2):64 * (hh % 2) + 64,
                                       128 * (q // 2):128 * (q // 2) + 128],
                                xtok[:, 64 * hloc:64 * (hloc + 1)],
                                sc4[bi][:, 128 * q:128 * (q + 1)],
                                start=True, stop=True,
                                tile_position=(0, 64 * (hh % 2))),
                                S["pe"] if q == 3 else None)
                        sc_tok[bi] = tky
                        if hb == 3:
                            tok_xtok_rd.append(tky)
                        v.wait(tky)
                        tyv = None
                        for q2 in range(2):
                            cc = h0 // 2 + q2
                            tyv = v.run(lambda e, cc=cc, q2=q2, bi=bi, c=c:
                                        e.scalar_tensor_tensor(
                                regv(cc, c * CS, CS), regv(cc, c * CS, CS),
                                dexp[:, cc:cc + 1],
                                py[bi][:, 128 * q2:128 * (q2 + 1)],
                                op0=ALU.mult, op1=ALU.add), S["dve"])
                        py_tok[bi] = tyv

            g.wait(S["bcg0"].cur())
            g.wait(S["bcg1"].cur())
            sy.wait(S["scs"].cur())
            emit_block(nc, E.values())

            E = engines()
            g, sy, v, s, t = E["gpsimd"], E["sync"], E["vector"], E["scalar"], E["tensor"]
            g.wait(S["dve"].cur())
            g.wait(S["act"].cur())
            g.run(lambda e: e.dma_start(d_ccin[0:128, :], sloc[:]), S["cci"], 16)
            g.run(lambda e: e.dma_start(
                d_ccin[128:129, 0:128], dloccol[:]), S["cci"], 16)
            t2 = None
            for zi in range(36):
                t2 = g.run(lambda e, zi=zi: e.dma_start(
                    d_ccin[128:129, 128 + 224 * zi:128 + 224 * (zi + 1)],
                    zrow[:]), S["cci"], 16)
            g.wait(t2)
            tcc = g.run(lambda e: e.collective_compute(
                "AllGather", ALU.bypass,
                replica_groups=[[0, 1, 2, 3], [4, 5, 6, 7]],
                ins=[d_ccin.opt()], outs=[d_ccout.opt()]), S["cc"], 1)
            g.wait(tcc)
            tok_stt = {}
            tgb = None
            for j in range(3):
                if tgb is not None:
                    g.wait(tgb)
                tdj = g.run(lambda e, j=j: e.dma_start(
                    gamcol[:], d_ccout[129 * j + 128:129 * j + 129, 0:128]),
                    S["gam"], 16)
                v.wait(tdj)
                v.run(lambda e, j=j: e.tensor_scalar(
                    gamcol[:], gamcol[:], 1.0, amask[:, j:j + 1],
                    op0=ALU.subtract, op1=ALU.mult))
                tgam = v.run(lambda e: e.tensor_scalar_add(gamcol[:], gamcol[:], 1.0),
                             S["dve"])
                tgb = colbcast(t, s, gamcol[:], in_tok=tgam)
                v.wait(tgb)
                tvp = v.run(lambda e: e.tensor_tensor(
                    prevb[:].rearrange("p (h q) -> p h q", h=128),
                    prevb[:].rearrange("p (h q) -> p h q", h=128),
                    bcb[:].unsqueeze(2).broadcast_to([128, 128, 64]),
                    op=ALU.mult), S["dve"])
                cb_state["bcb_rd"] = tvp
                for qq in range(4):
                    if tok_stt.get(qq % 2) is not None:
                        g.wait(tok_stt[qq % 2])
                    tsj = g.run(lambda e, j=j, qq=qq: e.dma_start(
                        sjp[qq % 2][:], d_ccout[129 * j:129 * j + 128,
                                                2048 * qq:2048 * (qq + 1)]),
                        S["sjp%d" % (qq % 2)], 16)
                    v.wait(tsj)
                    tok_stt[qq % 2] = v.run(lambda e, j=j, qq=qq: e.scalar_tensor_tensor(
                        prevb[:, 2048 * qq:2048 * (qq + 1)],
                        sjp[qq % 2][:], amask[:, j:j + 1],
                        prevb[:, 2048 * qq:2048 * (qq + 1)],
                        op0=ALU.mult, op1=ALU.add), S["dve"])
            g.wait(S["gam"].cur())
            g.wait(S["sjp0"].cur())
            g.wait(S["sjp1"].cur())
            emit_block(nc, E.values())

        with ExitStack() as ph:
            gateb = sb("a_gateb", [128, INTER], BF16, ph)
            cbuf = [sb(f"a_cbuf{i}", [128, 1024], BF16, ph) for i in range(2)]
            scbone = sb("a_scb", [128, 4096], BF16, ph)
            scb = [scbone, scbone]
            expck = sb("a_expck", [128, 128], F32, ph)
            y2s = sb("a_y2s", [128, 256], BF16, ph)
            cdbc = sb("a_cdbc", [128, 128], BF16, ph)
            rrow = sb("a_rrow", [1, 1024], F32, ph)
            rrowb = sb("a_rrowb", [1, 1024], BF16, ph)
            rstdb = sb("a_rstdb", [128, 1024], BF16, ph)

            pmisc3 = psum("p3_misc", [128, 512], F32, ph)
            pcm3 = pmisc3[:, 0:128]
            cb_state["pbc"] = pmisc3[:, 128:256]
            cb_state["prow"] = pmisc3[0:1, 384:512]
            cb_state["prow_rd"] = None
            cb_state["bcb_rd"] = None
            pa = [psum(f"p3_a{i}", [128, 256], F32, ph) for i in range(2)]
            pb = [psum(f"p3_b{i}", [128, 128], BF16, ph) for i in range(2)]
            pvar = [psum(f"p3_v{i}", [1, 512], F32, ph) for i in range(2)]
            prr = psum("p3_rr", [128, 512], F32, ph)

            E = engines()
            g, sy, v, s, t = E["gpsimd"], E["sync"], E["vector"], E["scalar"], E["tensor"]
            g.wait(S["scs"].cur())
            g.wait(S["act"].cur())
            g.wait(S["dve"].cur())
            g.wait(S["pe"].cur())
            t.wait(S["act"].cur())
            t.wait(S["dve"].cur())

            tok_gate_used = {}
            tok_c_used = {}
            tok_s_used = {}
            pa_tok = {}
            pb_tok = {}
            y2s_rd = None
            tok_expck_rd = None
            tok_cdbc_rd = None
            prev_trr = None
            prev_trs = None
            for c in range(NCHK):
                if c >= 1:
                    g.wait(tok_gate_used.get(c - 1))
                tok_gld = g.run(lambda e, c=c: e.dma_start(
                    gateb[:].rearrange("p (a b) -> p a b", a=NCC),
                    d_gate[c]), S["gld"], 16)
                if c >= 2:
                    g.wait(tok_c_used.get(c - 2))
                tok_cld = g.run(lambda e, c=c: e.dma_start(
                    cbuf[c % 2][:].rearrange("p (a b) -> p a b", a=8),
                    d_bc[c, :, 8:16, :]), S["cbf%d" % (c % 2)], 16)
                if c >= 1:
                    g.wait(tok_s_used.get(c - 1))
                tok_s0 = g.run(lambda e, c=c: e.dma_start(
                    scb[c % 2][:], d_sc[c, :, 0:4096]), S["scl"], 16)

                if tok_expck_rd is not None:
                    s.wait(tok_expck_rd)
                tc1 = t.run(lambda e, c=c: e.transpose(
                    pcm3, cumfm[:, c * CS:(c + 1) * CS], identf[:]), S["pe"])
                s.wait(tc1)
                tc2 = s.run(lambda e: e.activation(expck[:], pcm3, AF.Exp), S["act"])
                t.wait(tc2)

                tcb = colbcast(t, s, cdec[:, c:c + 1])
                v.wait(tcb)
                if tok_cdbc_rd is not None:
                    v.wait(tok_cdbc_rd)
                tcdbc = v.run(lambda e: e.tensor_copy(cdbc[:], bcb[:]), S["dve"])
                cb_state["bcb_rd"] = tcdbc

                last_mm = None
                for gg in range(8):
                    t.wait(tok_cld)
                    for hb in range(4):
                        bi = hb % 2
                        h0 = 16 * gg + 4 * hb
                        if pa_tok.get(bi) is not None:
                            t.wait(pa_tok[bi])
                        tk = None
                        for q in range(4):
                            hh = h0 + q
                            tk = t.run(lambda e, gg=gg, hh=hh, q=q, bi=bi, c=c: e.matmul(
                                pa[bi][:, 64 * q:64 * (q + 1)],
                                cbuf[c % 2][:, 128 * gg:128 * (gg + 1)],
                                prevb[:, 64 * hh:64 * (hh + 1)],
                                start=True, stop=True),
                                S["pe"] if q == 3 else None)
                        last_mm = tk
                        v.wait(tk)
                        v.wait(tc2)
                        if y2s_rd is not None:
                            v.wait(y2s_rd)
                        tvv = None
                        for q in range(4):
                            hh = h0 + q
                            tvv = v.run(lambda e, q=q, hh=hh, bi=bi: e.tensor_single_scalar(
                                y2s[:, 64 * q:64 * (q + 1)],
                                pa[bi][:, 64 * q:64 * (q + 1)],
                                expck[:, hh % 128:hh % 128 + 1], op=ALU.mult),
                                S["dve"] if q == 3 else None)
                        pa_tok[bi] = tvv
                        tok_expck_rd = tvv
                        t.wait(tvv)
                        tkb_last = None
                        for pr in range(2):
                            if pb_tok.get(pr) is not None:
                                t.wait(pb_tok[pr])
                            tkb = None
                            for q2 in range(2):
                                hh = h0 + 2 * pr + q2
                                tkb = t.run(lambda e, pr=pr, q2=q2: e.transpose(
                                    pb[pr][64 * q2:64 * (q2 + 1), :],
                                    y2s[:, 64 * (2 * pr + q2):64 * (2 * pr + q2 + 1)],
                                    identb[:], tile_position=(0, 64 * q2)),
                                    S["pe"] if q2 == 1 else None)
                            tkb_last = tkb
                            v.wait(tkb)
                            cc = (h0 + 2 * pr) // 2
                            tvb = v.run(lambda e, cc=cc, c=c, pr=pr: e.tensor_add(
                                regv(cc, c * CS, CS), regv(cc, c * CS, CS),
                                pb[pr][:]), S["dve"])
                            pb_tok[pr] = tvb
                        y2s_rd = tkb_last
                tok_c_used[c] = last_mm

                v.wait(tok_gld)
                tg1 = v.run(lambda e, c=c: e.tensor_mul(
                    regchunk(c), regchunk(c),
                    gateb[:].rearrange("p (a b) -> p a b", a=NCC)), S["dve"])
                s.wait(tg1)
                tg2 = s.run(lambda e, c=c: e.activation(
                    gateb[:].rearrange("p (a b) -> p a b", a=NCC),
                    regchunk(c), AF.Square), S["act"])
                t.wait(tg2)
                if prev_trr is not None:
                    t.wait(prev_trr)
                tvm3 = tvm7 = None
                for grp in range(8):
                    tkv = None
                    for qq in range(8):
                        cc = 8 * grp + qq
                        tkv = t.run(lambda e, grp=grp, cc=cc, qq=qq: e.matmul(
                            pvar[grp // 4][0:1, 128 * (grp % 4):128 * (grp % 4) + 128],
                            onescolb[:], gateb[:, cc * 128:(cc + 1) * 128],
                            start=(qq == 0), stop=(qq == 7)),
                            S["pe"] if qq == 7 else None)
                    if grp == 3:
                        tvm3 = tkv
                    if grp == 7:
                        tvm7 = tkv
                tok_gate_used[c] = tvm7
                s.wait(tvm3)
                s.wait(tvm7)
                s.run(lambda e: e.activation(rrow[0:1, 0:512], pvar[0][:], AF.Ln,
                                             bias=rmseps[:], scale=1.0 / 1024.0))
                s.run(lambda e: e.activation(rrow[0:1, 512:1024], pvar[1][:], AF.Ln,
                                             bias=rmseps[:], scale=1.0 / 1024.0))
                trr = s.run(lambda e: e.activation(rrow[:], rrow[:], AF.Exp,
                                                   scale=-0.5), S["act"])
                prev_trr = trr
                v.wait(trr)
                trb = v.run(lambda e: e.tensor_copy(rrowb[:], rrow[:]), S["dve"])
                t.wait(trb)
                if prev_trs is not None:
                    t.wait(prev_trs)
                tbc1 = t.run(lambda e: e.matmul(prr[:], onesb[:], rrowb[0:1, 0:512],
                                                start=True, stop=True), S["pe"])
                s.wait(tbc1)
                trs1 = s.run(lambda e: e.copy(rstdb[:, 0:512], prr[:]), S["act"])
                t.wait(trs1)
                tbc2 = t.run(lambda e: e.matmul(prr[:], onesb[:],
                                                rrowb[0:1, 512:1024],
                                                start=True, stop=True), S["pe"])
                s.wait(tbc2)
                trs = s.run(lambda e: e.copy(rstdb[:, 512:1024], prr[:]), S["act"])
                prev_trs = trs
                v.wait(trs)
                for grp in range(8):
                    v.run(lambda e, grp=grp, c=c: e.tensor_tensor(
                        regchunk(c)[:, 8 * grp:8 * (grp + 1), :],
                        regchunk(c)[:, 8 * grp:8 * (grp + 1), :],
                        normw[:, 8 * grp:8 * (grp + 1)].unsqueeze(2)
                        .broadcast_to([128, 8, CS]), op=ALU.mult))
                    v.run(lambda e, grp=grp, c=c: e.tensor_tensor(
                        regchunk(c)[:, 8 * grp:8 * (grp + 1), :],
                        regchunk(c)[:, 8 * grp:8 * (grp + 1), :],
                        rstdb[:, 128 * grp:128 * (grp + 1)].unsqueeze(1)
                        .broadcast_to([128, 8, CS]), op=ALU.mult))

                v.wait(tcdbc)
                v.wait(last_mm)
                tp1 = v.run(lambda e: e.tensor_tensor(
                    prevb[:].rearrange("p (h q) -> p h q", h=128),
                    prevb[:].rearrange("p (h q) -> p h q", h=128),
                    cdbc[:].unsqueeze(2).broadcast_to([128, 128, 64]),
                    op=ALU.mult), S["dve"])
                tok_cdbc_rd = tp1
                v.wait(tok_s0)
                tp2 = v.run(lambda e, c=c: e.tensor_add(
                    prevb[:, 0:4096], prevb[:, 0:4096], scb[c % 2][:]), S["dve"])
                g.wait(tp2)
                tok_s1 = g.run(lambda e, c=c: e.dma_start(
                    scb[c % 2][:], d_sc[c, :, 4096:8192]), S["scl"], 16)
                v.wait(tok_s1)
                tp3 = v.run(lambda e, c=c: e.tensor_add(
                    prevb[:, 4096:8192], prevb[:, 4096:8192], scb[c % 2][:]),
                    S["dve"])
                tok_s_used[c] = tp3

            for k in ["gld", "cbf0", "cbf1", "scl"]:
                g.wait(S[k].cur())
            emit_block(nc, E.values())

        with ExitStack() as ph:
            bwt = sb("a_bwt", [128, HID], BF16, ph)
            wos = [sb(f"a_wos{i}", [128, 512], BF16, ph) for i in range(2)]
            xres = [sb(f"a_xr{i}", [128, 512], F32, ph) for i in range(2)]
            ostg = [sb(f"a_os{i}", [128, 512], F32, ph) for i in range(2)]
            pout = [psum(f"p4_o{i}", [128, 512], F32, ph) for i in range(8)]

            E = engines()
            g, sy, v, s, t = E["gpsimd"], E["sync"], E["vector"], E["scalar"], E["tensor"]
            g.wait(S["act"].cur())
            g.wait(S["dve"].cur())
            g.wait(S["pe"].cur())
            t.wait(S["act"].cur())
            t.wait(S["dve"].cur())

            tbw = g.run(lambda e: e.dma_start(bwt[:], d_bwt), S["bwl"], 16)
            t.wait(tbw)
            tok_ev4 = {}
            tok_st4 = {}
            tok_mmk = {}
            for d in range(8):
                for tt in range(8):
                    if d >= 1:
                        t.wait(tok_ev4[(d - 1, tt)])
                    t.run(lambda e, d=d, tt=tt: e.matmul(
                        pout[tt][:], whfm[:, 128 * tt:128 * (tt + 1)],
                        bwt[:, 512 * d:512 * (d + 1)], start=True, stop=False))
                for k in range(64):
                    ki = 64 * d + k
                    if ki >= 2:
                        g.wait(tok_mmk.get(ki - 2))
                    twk = g.run(lambda e, d=d, k=k: e.dma_start(
                        wos[k % 2][:], d_wob[k][:, 512 * d:512 * (d + 1)]),
                        S["wos%d" % (k % 2)], 16)
                    t.wait(twk)
                    tk = None
                    for tt in range(8):
                        tk = t.run(lambda e, d=d, k=k, tt=tt: e.matmul(
                            pout[tt][:],
                            region[:, k * TOK + 128 * tt:k * TOK + 128 * (tt + 1)],
                            wos[k % 2][:], start=False, stop=(k == 63)),
                            S["pe"] if tt == 7 else None)
                    tok_mmk[ki] = tk
                for tt in range(8):
                    di = 8 * d + tt
                    if di >= 2:
                        prevtok = tok_ev4[(d, tt - 2)] if tt >= 2 else tok_ev4[(d - 1, 6 + tt)]
                        g.wait(prevtok)
                    txl = g.run(lambda e, d=d, tt=tt: e.dma_start(
                        xres[tt % 2][:], d_x[3 + 128 * tt:3 + 128 * (tt + 1),
                                            512 * d:512 * (d + 1)]),
                        S["xr%d" % (tt % 2)], 16)
                    v.wait(tok_mmk[64 * d + 63])
                    v.wait(txl)
                    if di >= 2:
                        prevst = tok_st4[(d, tt - 2)] if tt >= 2 else tok_st4[(d - 1, 6 + tt)]
                        v.wait(prevst)
                    tev = v.run(lambda e, d=d, tt=tt: e.tensor_add(
                        ostg[tt % 2][:], pout[tt][:], xres[tt % 2][:]), S["dve"])
                    tok_ev4[(d, tt)] = tev
                    sy.wait(tev)
                    tok_st4[(d, tt)] = sy.run(lambda e, d=d, tt=tt: e.dma_start(
                        d_out[128 * tt:128 * (tt + 1), 512 * d:512 * (d + 1)],
                        ostg[tt % 2][:]), S["out%d" % (tt % 2)], 16)

            for k in ["bwl", "wos0", "wos1", "xr0", "xr1"]:
                g.wait(S[k].cur())
            sy.wait(S["out0"].cur())
            sy.wait(S["out1"].cur())
            emit_block(nc, E.values())
        p23.close()
    return nc


def kernel(L0_attn_input_states, L0_attn_in_proj_w, L0_attn_conv_w,
           L0_attn_conv_b, L0_attn_dt_bias, L0_attn_A_log, L0_attn_D,
           L0_attn_norm_w, L0_attn_out_w, L0_ffn_route_w, L0_ffn_A, L0_ffn_Bw):
    x = np.asarray(L0_attn_input_states, np.float32)
    W = np.asarray(L0_attn_in_proj_w, np.float32)
    Wo = np.asarray(L0_attn_out_w, np.float32)
    cw = np.asarray(L0_attn_conv_w, np.float32)
    cb = np.asarray(L0_attn_conv_b, np.float32)
    dtb = np.asarray(L0_attn_dt_bias, np.float32)
    alog = np.asarray(L0_attn_A_log, np.float32)
    Dv = np.asarray(L0_attn_D, np.float32)
    nw = np.asarray(L0_attn_norm_w, np.float32)
    rw = np.asarray(L0_ffn_route_w, np.float32)
    Am = np.asarray(L0_ffn_A, np.float32)
    Bw = np.asarray(L0_ffn_Bw, np.float32)

    wt = np.ascontiguousarray(W.reshape(NCH, 128, NKT, 128).transpose(0, 3, 2, 1))
    wo = np.ascontiguousarray(Wo.T.reshape(64, 128, HID))
    convw = np.ascontiguousarray(
        cw[:, 0, :].reshape(NXBC, 128, KCONV).transpose(1, 0, 2).reshape(128, NXBC * KCONV))
    convb = np.ascontiguousarray(cb.reshape(NXBC, 128).T)
    dexp = np.ascontiguousarray(np.repeat(Dv.reshape(NCC, 2), 64, axis=1).T)
    normw = np.ascontiguousarray(nw.reshape(NCC, 128).T)
    routew = np.ascontiguousarray(rw.T.reshape(NKT, 128, 8))
    a_t = np.ascontiguousarray(Am.reshape(128, HID).T.reshape(NKT, 128, 128))
    bw_t = np.ascontiguousarray(Bw.transpose(0, 2, 1).reshape(128, HID))
    identb = np.eye(128, dtype=ml_dtypes.bfloat16)
    identf = np.eye(128, dtype=np.float32)
    jj, ii = np.meshgrid(np.arange(128), np.arange(128), indexing="ij")
    trilb = (jj <= ii).astype(ml_dtypes.bfloat16)
    mask01 = (ii >= jj).astype(np.float32)

    xf = x.reshape(2 * 4096, HID)
    in_maps = []
    for core in range(8):
        b, q = core // 4, core % 4
        lo = b * 4096 + q * 1024
        halo = np.zeros((3, HID), np.float32) if q == 0 else xf[lo - 3:lo]
        x_ext = np.ascontiguousarray(np.concatenate([halo, xf[lo:lo + 1024]], 0))
        am = np.zeros((128, 4), np.float32)
        am[:, :q] = 1.0
        in_maps.append(dict(
            x_ext=x_ext, wt=wt, wo=wo, convw=convw, convb=convb,
            dtb=np.ascontiguousarray(dtb.reshape(128, 1)),
            alog=np.ascontiguousarray(alog.reshape(128, 1)),
            dexp=dexp, normw=normw, routew=routew, a_t=a_t, bw_t=bw_t,
            identb=identb, identf=identf, trilb=trilb, mask01=mask01,
            amask=np.ascontiguousarray(am)))

    nc = build_program()
    res = run_bass_kernel_spmd(nc, in_maps, core_ids=list(range(8)))
    out = np.empty((2, 4096, HID), np.float32)
    for core in range(8):
        b, q = core // 4, core % 4
        out[b, q * 1024:(q + 1) * 1024, :] = res.results[core]["out"]
    return out
